# version 19 (final)
# baseline (speedup 1.0000x reference)
import numpy as np
import ml_dtypes
import concourse.bass as bass
import concourse.mybir as mybir
from concourse.bass_utils import run_bass_kernel_spmd
from contextlib import ExitStack

F32 = mybir.dt.float32
BF16 = mybir.dt.bfloat16
AF = mybir.ActivationFunctionType
ALU = mybir.AluOpType
AX = mybir.AxisListType
BF = ml_dtypes.bfloat16

S = 2048
D = 2048
NH = 16
HD = 128
NI = 1024
NE = 8
NT = 16
NCH = 16
SCALE = 0.08838834764831845
EPS = 1e-5

DBG_SKIP_NORM = False

ENGINES = ("sync", "gpsimd", "tensor", "vector", "scalar")
OWNER = {"t": "tensor", "a": "scalar", "v": "vector"}


def build_nc(debug=False):
    nc = bass.Bass("TRN2", target_bir_lowering=False)

    def din(name, shape, dt=F32):
        return nc.dram_tensor(name, shape, dt, kind="ExternalInput").ap()
    x_d = din("x", [S, D])
    wqkv_d = din("wqkv", [NH, 128, 16 * 384])
    bqkv_d = din("bqkvt", [128, 48])
    wout_d = din("woutt", [128, NCH * D])
    bout_d = din("bout", [1, D])
    gwt_d = din("gwt", [128, 128])
    wg_d = din("wgt", [NE, 128, NCH * NI])
    wu_d = din("wut", [NE, 128, NCH * NI])
    wd_d = din("wdt", [NE, 128, 8 * D])
    masks_d = din("masks", [128, 4 * 512], BF16)
    onesc_d = din("ones_col", [128, 1], BF16)
    onesr_d = din("ones_row", [1, 512], BF16)
    onessq_d = din("ones_sq", [128, 128], BF16)
    out_d = nc.dram_tensor("out", [S, D], F32, kind="ExternalOutput").ap()
    dbg = {}
    if debug:
        for nm, sh, dt in [("d_x1T", [128, NCH * S], BF16),
                           ("d_xT", [128, NCH * S], BF16),
                           ("d_qT", [128, S], BF16), ("d_kT", [128, S], BF16),
                           ("d_vT", [128, S], BF16), ("d_v0", [128, S], BF16),
                           ("d_ctx", [128, NH * S], BF16),
                           ("d_x1f", [S, D], F32), ("d_x1bf", [S, D], BF16),
                           ("d_el", [128, NT * 8], F32), ("d_call", [128, NT * 8], F32),
                           ("d_at0", [8, 128, S], BF16),
                           ("d_res0", [128, D], F32), ("d_xs0", [128, D], F32),
                           ("d_acc", [128, NT * D], BF16)]:
            dbg[nm] = nc.dram_tensor(nm, sh, dt, kind="ExternalOutput").ap()

    ident_d = din("ident", [128, 128])
    identb_d = din("identb", [128, 128], BF16)
    vt_scr = nc.dram_tensor("vt_scr", [2, 128, S], BF16, kind="Internal").ap()
    x1f_scr = nc.dram_tensor("x1f_scr", [S, D], F32, kind="Internal").ap()
    x1bf_scr = nc.dram_tensor("x1bf_scr", [S, D], BF16, kind="Internal").ap()
    ats_scr = nc.dram_tensor("ats_scr", [NE, 8, 128, S], BF16, kind="Internal").ap()
    wout_scr = nc.dram_tensor("wout_scr", [128, NCH * D], BF16, kind="Internal").ap()
    wg0_scr = nc.dram_tensor("wg0_scr", [128, NCH * NI], BF16, kind="Internal").ap()
    wu0_scr = nc.dram_tensor("wu0_scr", [128, NCH * NI], BF16, kind="Internal").ap()
    wd0_scr = nc.dram_tensor("wd0_scr", [128, 8 * D], BF16, kind="Internal").ap()

    cnt = {}
    water = {e: {} for e in ENGINES}
    ops = []
    sems = {}

    pend_after_te = []

    dummy_flush = {}

    def em(eng, fn, waits=(), sem=None, inc=1):
        ws = []
        for w in waits:
            if w is None:
                continue
            sname, val = w
            if OWNER.get(sname) == eng:
                continue
            ws.append((sname, val))
        mil = None
        if sem in ("a", "v") and fn is not None:
            ops.append((eng, fn, None, 0, ws))
            fn = dummy_flush[eng]
            ws = []
        sem = {"ar": "a", "vr": "v"}.get(sem, sem)
        if sem is not None:
            cnt[sem] = cnt.get(sem, 0) + inc
            mil = (sem, cnt[sem])
        ops.append((eng, fn, sem, inc if sem is not None else 0, ws))
        if eng == "tensor" and sem == "t" and pend_after_te:
            thunks = list(pend_after_te)
            pend_after_te.clear()
            for th in thunks:
                th(mil)
        return mil

    def defer_te(thunk):
        pend_after_te.append(thunk)

    def te_tail():
        if pend_after_te:
            em("tensor", lambda e: e.drain(), [], "t", 1)

    def dma(eng, out, in_, waits=(), transpose=False):
        sname = {"sync": "ds", "gpsimd": "dg", "scalar": "dc"}[eng]
        if transpose:
            fn = lambda e, o=out, i=in_: e.dma_start_transpose(o, i)
        else:
            fn = lambda e, o=out, i=in_: e.dma_start(o, i)
        return em(eng, fn, waits, sname, 16)

    def barrier():
        te_tail()
        for e in ENGINES:
            w = []
            if e == "sync":
                w = [("ds", cnt.get("ds", 0))]
            if e == "gpsimd":
                w = [("dg", cnt.get("dg", 0)), ("do", cnt.get("do", 0))]
            if e == "scalar":
                w = [("dc", cnt.get("dc", 0))]
            em(e, lambda eng: eng.nop(), w, "bar", 1)
        target = cnt["bar"]
        for e in ENGINES:
            em(e, None, [("bar", target)])

    def flush():
        with nc.Block() as blk:
            for en in ENGINES:
                def make(en):
                    def run(e):
                        wm = water[en]
                        for (oeng, fn, sem, inc, ws) in ops:
                            if oeng != en:
                                continue
                            for (sname, val) in ws:
                                if wm.get(sname, 0) >= val:
                                    continue
                                e.wait_ge(sems[sname], val)
                                wm[sname] = val
                            if fn is None:
                                continue
                            ins = fn(e)
                            if sem is not None:
                                ins.then_inc(sems[sem], inc)
                    return run
                getattr(blk, en)(make(en))
        ops.clear()

    with ExitStack() as g:
        for sname in ("bar", "ds", "dg", "dc", "do", "t", "a", "v"):
            sems[sname] = g.enter_context(nc.semaphore(sname))
        SB = lambda st, *a: st.enter_context(nc.sbuf_tensor(*a))
        PS = lambda st, *a: st.enter_context(nc.psum_tensor(*a))

        smf = SB(g, "smallf", [128, 344], F32)
        _fo = [0]
        def carvef(n):
            a = _fo[0]
            _fo[0] += n
            return smf[:, a:a + n]
        dmv = carvef(2)
        dma_t = carvef(2)
        bqkvt = carvef(48)
        c_all = carvef(NT * 8)
        el_all = carvef(NT * 8)
        sm = {k: carvef(2)
              for k in ("sum_x", "ssq", "mvec", "msq", "t1v", "varv", "lnv", "rstd")}
        t8a = carvef(8)
        t8b = carvef(8)
        m1t = carvef(1)
        m2t = carvef(1)
        s12t = carvef(1)
        rsv = carvef(1)
        smb = SB(g, "smallb", [128, 2308], BF16)
        _bo = [0]
        def carveb(n):
            a = _bo[0]
            _bo[0] += n
            return smb[:, a:a + n]
        masks = carveb(4 * 512)
        onesc = carveb(1)
        onessq = carveb(128)
        gwt = carveb(128)
        dummy_flush["vector"] = lambda e: e.tensor_copy(dmv[:, 0:1], dmv[:, 1:2])
        dummy_flush["scalar"] = lambda e: e.copy(dma_t[:, 0:1], dma_t[:, 1:2])
        onesr = SB(g, "onesr_sb", [1, 512], BF16)
        bout = SB(g, "bout_sb", [1, D], BF16)

        with ExitStack() as pab:
            ctx_all = SB(pab, "ctx_all", [128, NH * S], BF16)
            with ExitStack() as pa:
                xT = SB(pa, "xT", [128, NCH * S], BF16)
                xfs = [SB(pa, f"xfs{i}", [128, D], F32) for i in range(2)]
                identsb = SB(pa, "identsb", [128, 128], F32)
                wbuf = [SB(pa, f"wbuf{i}", [128, 16 * 384], BF16) for i in range(2)]
                qT = SB(pa, "qT", [128, S], BF16)
                kT = SB(pa, "kT", [128, S], BF16)
                vT = SB(pa, "vT", [128, S], BF16)
                v_sb = [SB(pa, f"v_sb{i}", [128, S], BF16) for i in range(2)]
                p_sb = [SB(pa, f"p_sb{i}", [128, 512], BF16) for i in range(4)]
                lnden = SB(pa, "lnden", [128, 512], F32)
                rdbc = SB(pa, "rdbc", [128, 512], F32)

                pool = [PS(pa, f"pool{i}", [128, 512], F32) for i in range(4)]
                ctx_pp = [PS(pa, f"ctx_pp{i}", [128, 512], F32) for i in range(2)]
                den_pp = [PS(pa, f"den_pp{i}", [128, 512], F32) for i in range(2)]

                dma("sync", masks[:], masks_d)
                dma("sync", onesc[:], onesc_d)
                dma("sync", onesr[:], onesr_d)
                dma("sync", onessq[:], onessq_d)
                dma("sync", bqkvt[:], bqkv_d)
                m_gwt = dma("gpsimd", gwt[:], gwt_d)
                m_bout = dma("gpsimd", bout[:], bout_d)

                m_w = [None] * NH
                m_w[0] = dma("gpsimd", wbuf[0][:], wqkv_d[0])
                m_w[1] = dma("gpsimd", wbuf[1][:], wqkv_d[1])

                pool_war = [None] * 4
                pool_ctr = [0]
                m_ident = dma("sync", identsb[:], ident_d)
                m_xfr = [None] * NT
                m_xev_last = {}
                q_gate = [None] * 4

                def emit_xread(tt):
                    w = []
                    if tt >= 2:
                        w += list(m_xev_last[tt - 2].values())
                    m_xfr[tt] = dma("sync" if tt % 2 == 0 else "scalar",
                                    xfs[tt % 2][:],
                                    x_d[tt * 128:(tt + 1) * 128, :], w)

                emit_xread(0)
                emit_xread(1)
                for tt in range(NT):
                    m_xev_last[tt] = {}
                    for grp in range(4):
                        slot = pool_ctr[0] % 4
                        pool_ctr[0] += 1
                        mm = None
                        for k in range(4):
                            ch = grp * 4 + k
                            w = [pool_war[slot]] if k == 0 else []
                            if grp == 0 and k == 0:
                                w.append(m_xfr[tt])
                                if tt == 0:
                                    w.append(m_ident)
                            def ftr(e, tt=tt, ch=ch, k=k, slot=slot):
                                return e.matmul(pool[slot][:, k * 128:(k + 1) * 128],
                                                xfs[tt % 2][:, ch * 128:(ch + 1) * 128],
                                                identsb[:], start=True, stop=True)
                            mm = em("tensor", ftr, w, "t", 1)
                        lt = ("t", mm[1] + 1)
                        dst = xT.rearrange("p (c s) -> p c s", s=S)[
                            :, grp * 4:(grp + 1) * 4, tt * 128:(tt + 1) * 128]
                        if grp % 2 == 0:
                            def fe(e, dst=dst, slot=slot):
                                return e.activation(
                                    dst, pool[slot][:].rearrange("p (c k) -> p c k", k=128),
                                    AF.Identity)
                            mev = em("scalar", fe, [lt], "a", 1)
                            m_xev_last[tt]["a"] = mev
                        else:
                            def fe(e, dst=dst, slot=slot):
                                return e.tensor_copy(
                                    dst, pool[slot][:].rearrange("p (c k) -> p c k", k=128))
                            mev = em("vector", fe, [lt], "v", 1)
                            m_xev_last[tt]["v"] = mev
                        pool_war[slot] = mev
                    if tt + 2 < NT:
                        emit_xread(tt + 2)
                em("tensor", lambda e: e.drain(), [], "t", 1)
                for q in range(4):
                    g = []
                    for tt in range(4 * q, 4 * q + 4):
                        g += list(m_xev_last[tt].values())
                    q_gate[q] = g
                m_xT = q_gate[3]

                last_ctx_consumer = [None, None]
                last_den_consumer = [None, None]
                last_rdbc = [None]
                m_vscr_w = [[None] * 4, [None] * 4]
                m_vread = [None, None]
                m_vreadc = [None, None]
                last_ctxread = [None, None]
                p_war = [None] * 4
                m_denlast = {}
                m_ctx_stop = {}
                m_qkv_ev = {}
                m_qkv_done = {}
                sc_j = [0]

                m_dbg = {}
                for h in range(NH):
                    if debug and h == 0:
                        m_dbg["xT"] = dma("gpsimd", dbg["d_xT"], xT[:], m_xT)
                    if debug and h == 1:
                        m_dbg["q"] = dma("gpsimd", dbg["d_qT"], qT[:], [m_qkv_ev[(0, 11)]])
                        m_dbg["k"] = dma("gpsimd", dbg["d_kT"], kT[:], [m_qkv_ev[(0, 11)]])
                        m_dbg["v"] = dma("gpsimd", dbg["d_vT"], vT[:], [m_qkv_ev[(0, 11)]])
                        m_dbg["v0"] = dma("gpsimd", dbg["d_v0"], v_sb[0][:], [m_vread[0]])
                    if 2 <= h <= 5:
                        k = h - 2
                        dma("gpsimd", wout_scr[:, k * 8192:(k + 1) * 8192],
                            wout_d[:, k * 8192:(k + 1) * 8192])
                    if 6 <= h <= 9:
                        for i in range(4):
                            c = 4 * (h - 6) + i
                            dma("gpsimd", wg0_scr[:, c * NI:(c + 1) * NI],
                                wg_d[0][:, c * NI:(c + 1) * NI])
                    if 10 <= h <= 13:
                        for i in range(4):
                            c = 4 * (h - 10) + i
                            dma("gpsimd", wu0_scr[:, c * NI:(c + 1) * NI],
                                wu_d[0][:, c * NI:(c + 1) * NI])
                    grp_m = []
                    for gi in range(12):
                        op = (2, 0, 1)[gi // 4]
                        tc = gi % 4
                        slot = pool_ctr[0] % 4
                        pool_ctr[0] += 1
                        ps = pool[slot]
                        w = [m_w[h], pool_war[slot]]
                        if debug and h == 1 and gi == 0:
                            w += [m_dbg["q"], m_dbg["k"], m_dbg["v"], m_dbg["v0"]]
                        if h == 0 and gi < 4:
                            w += q_gate[tc]
                        mm = None
                        for ch in range(NCH):
                            def f(e, ps=ps, ch=ch, op=op, tc=tc, h=h):
                                return e.matmul(
                                    ps[:],
                                    wbuf[h % 2][:, ch * 384 + op * 128: ch * 384 + (op + 1) * 128],
                                    xT[:, ch * S + tc * 512: ch * S + (tc + 1) * 512],
                                    start=(ch == 0), stop=(ch == NCH - 1))
                            mm = em("tensor", f, w if ch == 0 else [],
                                    "t" if ch in (0, NCH - 1) else None, 1)
                        grp_m.append(mm)
                        dest = (qT, kT, vT)[op]
                        gcol = 3 * h + op
                        def fe(e, ps=ps, dest=dest, tc=tc, gcol=gcol):
                            return e.activation(dest[:, tc * 512:(tc + 1) * 512], ps[:],
                                                AF.Identity, bias=bqkvt[:, gcol:gcol + 1],
                                                scale=1.0)
                        evw = [mm]
                        if op == 2 and h > 0:
                            evw.append(m_vscr_w[(h - 1) % 2][tc])
                        def th_ev(mil, fe=fe, evw=evw, h=h, gi=gi, slot=slot,
                                  op=op, tc=tc):
                            style = "a" if gi == 11 else "ar"
                            mev = em("scalar", fe, evw + [mil], style, 1)
                            m_qkv_ev[(h, gi)] = mev
                            pool_war[slot] = mev
                            m_qkv_done[(h, gi - 1)] = mev
                            if gi == 11:
                                m_qkv_done[(h, 11)] = mev
                            if 1 <= gi <= 4:
                                btc = gi - 1
                                wv = [mev]
                                if m_vread[h % 2] is not None:
                                    wv.append(m_vread[h % 2])
                                    wv.append(m_vreadc[h % 2])
                                m_vscr_w[h % 2][btc] = dma(
                                    "sync", vt_scr[h % 2][:, btc * 512:(btc + 1) * 512],
                                    vT[:, btc * 512:(btc + 1) * 512], wv)
                        defer_te(th_ev)

                    mv_s = None
                    mv_c = None
                    for kt in range(NT):
                        w = [m_vscr_w[h % 2][kt // 4]]
                        if kt < 2 and last_ctxread[h % 2] is not None:
                            w.append(last_ctxread[h % 2])
                        mv_s = dma("sync", v_sb[h % 2][:, kt * 128:(kt + 1) * 128],
                                   vt_scr[h % 2][:, kt * 128:(kt + 1) * 128], w,
                                   transpose=True)
                    m_vread[h % 2] = mv_s
                    m_vreadc[h % 2] = mv_s
                    m_vread2 = mv_s

                    steps = [(qc, kt) for qc in range(4) for kt in range(4 * qc + 4)]
                    NSTEP = len(steps)
                    m_exp = {}
                    m_mask = {}
                    s_slot = {}

                    def is_diag(j):
                        qc, kt = steps[j]
                        return kt >= 4 * qc

                    m_s_t = {}

                    def emit_S(j, h=h, mv_c=m_vread2):
                        qc, kt = steps[j]
                        slot = pool_ctr[0] % 4
                        pool_ctr[0] += 1
                        s_slot[j] = slot
                        w = [m_qkv_done[(h, 4 + qc)], m_qkv_done[(h, 8 + kt // 4)],
                             pool_war[slot]]
                        if kt == 0:
                            if last_ctx_consumer[qc % 2] is not None:
                                w.append(last_ctx_consumer[qc % 2])
                            if last_den_consumer[qc % 2] is not None:
                                w.append(last_den_consumer[qc % 2])
                        if j == 0:
                            w.append(m_vread[h % 2])
                            w.append(mv_c)
                        def fs(e, kt=kt, qc=qc, slot=slot):
                            return e.matmul(pool[slot][:],
                                            kT[:, kt * 128:(kt + 1) * 128],
                                            qT[:, qc * 512:(qc + 1) * 512],
                                            start=True, stop=True)
                        m_s_t[j] = em("tensor", fs, w, "t", 1)

                    def emit_exp(j):
                        lt = ("t", m_s_t[j][1] + 1)
                        slot = s_slot[j]
                        w = [lt, p_war[j % 4]]
                        def fx(e, j=j, slot=slot):
                            return e.activation(p_sb[j % 4][:], pool[slot][:],
                                                AF.Exp, scale=SCALE)
                        style = "a" if is_diag(j) else "ar"
                        m_exp[j] = em("scalar", fx, w, style, 1)
                        pool_war[slot] = m_exp[j]
                        if is_diag(j):
                            emit_mask(j)

                    def emit_mask(j):
                        qc, kt = steps[j]
                        r = kt - 4 * qc
                        def fmk(e, j=j, r=r):
                            return e.tensor_mul(p_sb[j % 4][:], p_sb[j % 4][:],
                                                masks[:, r * 512:(r + 1) * 512])
                        m_mask[j] = em("vector", fmk, [m_exp[j]], "v", 1)

                    def emit_cd(j, h=h):
                        qc, kt = steps[j]
                        nkt = 4 * qc + 4
                        pw = [m_mask[j]] if is_diag(j) else [m_exp[j + 1]]
                        def fc(e, kt=kt, qc=qc, j=j, h=h):
                            return e.matmul(ctx_pp[qc % 2][:],
                                            v_sb[h % 2][:, kt * 128:(kt + 1) * 128],
                                            p_sb[j % 4][:],
                                            start=(kt == 0), stop=(kt == nkt - 1))
                        mc = em("tensor", fc, pw, "t", 1)
                        def fd(e, kt=kt, qc=qc, j=j):
                            return e.matmul(den_pp[qc % 2][:], onessq[:],
                                            p_sb[j % 4][:],
                                            start=(kt == 0), stop=(kt == nkt - 1))
                        md = em("tensor", fd, [], "t", 1)
                        p_war[j % 4] = md
                        if kt == nkt - 1:
                            m_ctx_stop[(h, qc)] = mc
                            m_denlast[(h, qc)] = md
                            last_ctxread[h % 2] = mc
                            def th_epi(mil, h=h, qc=qc):
                                w_rd = [m_denlast[(h, qc)], mil]
                                if last_rdbc[0] is not None:
                                    w_rd.append(last_rdbc[0])
                                def fr(e, qc=qc):
                                    return e.reciprocal(rdbc[:], den_pp[qc % 2][:])
                                m_rd = em("vector", fr, w_rd, "v", 1)
                                last_den_consumer[qc % 2] = m_rd
                                def fm(e, h=h, qc=qc):
                                    return e.tensor_tensor(
                                        ctx_all[:, h * S + qc * 512: h * S + (qc + 1) * 512],
                                        ctx_pp[qc % 2][:], rdbc[:], ALU.mult)
                                mfm = em("vector", fm,
                                         [m_ctx_stop[(h, qc)]], "v", 1)
                                last_ctx_consumer[qc % 2] = mfm
                                last_rdbc[0] = mfm
                            defer_te(th_epi)

                    emit_S(0)
                    emit_S(1)
                    emit_S(2)
                    emit_exp(0)
                    emit_S(3)
                    emit_exp(1)
                    for i in range(NSTEP):
                        emit_cd(i)
                        if i + 4 < NSTEP:
                            emit_S(i + 4)
                        if i + 2 < NSTEP:
                            emit_exp(i + 2)

                    if h + 2 < NH:
                        m_w[h + 2] = dma("gpsimd", wbuf[h % 2][:], wqkv_d[h + 2],
                                         [grp_m[-1]])

                barrier()
                flush()

            with ExitStack() as pb:
                wout_sb = SB(pb, "wout_sb", [128, NCH * D], BF16)
                x_sb = [SB(pb, f"x_sb{i}", [128, D], F32) for i in range(2)]
                resid = [SB(pb, f"resid{i}", [128, D], F32) for i in range(2)]
                x1f = [SB(pb, f"x1f{i}", [128, D], F32) for i in range(2)]
                x1bf = [SB(pb, f"x1bf{i}", [128, D], BF16) for i in range(2)]
                out_pp = [PS(pb, f"out_pp{i}", [128, D], F32) for i in range(2)]

                if debug:
                    dma("gpsimd", dbg["d_ctx"], ctx_all[:])
                for ic in range(8):
                    dma("gpsimd", wd0_scr[:, ic * D:(ic + 1) * D],
                        wd_d[0][:, ic * D:(ic + 1) * D])
                m_wout_k = [None] * 16
                for k in range(16):
                    m_wout_k[k] = dma("sync" if k % 2 == 0 else "scalar",
                                      wout_sb[:, k * 2048:(k + 1) * 2048],
                                      wout_scr[:, k * 2048:(k + 1) * 2048])
                m_x = [None] * NT
                m_x1f_w = [None] * NT
                m_x1bf_w = [None] * NT
                m_resid = [None] * NT
                m_norm = [None] * NT
                m_op = {}
                for tt in range(2):
                    m_x[tt] = dma("sync", x_sb[tt % 2][:],
                                  x_d[tt * 128:(tt + 1) * 128, :])
                for tt in range(NT):
                    for dc in range(4):
                        w = []
                        if dc == 0 and tt >= 2:
                            w.append(m_resid[tt - 2])
                        mm = None
                        for cc in range(NCH):
                            if cc == 0:
                                wc = list(w)
                            else:
                                wc = []
                            if tt == 0:
                                wc.append(m_wout_k[cc])
                            def f(e, tt=tt, dc=dc, cc=cc):
                                return e.matmul(
                                    out_pp[tt % 2][:, dc * 512:(dc + 1) * 512],
                                    ctx_all[:, cc * S + tt * 128: cc * S + tt * 128 + 128],
                                    wout_sb[:, cc * D + dc * 512: cc * D + (dc + 1) * 512],
                                    start=(cc == 0), stop=False)
                            mm = em("tensor", f, wc,
                                    "t" if cc == 0 else None, 1)
                        def fb(e, tt=tt, dc=dc):
                            return e.matmul(out_pp[tt % 2][:, dc * 512:(dc + 1) * 512],
                                            onesr[0:1, 0:128],
                                            bout[0:1, dc * 512:(dc + 1) * 512],
                                            start=False, stop=True)
                        m_op[(tt, dc)] = em("tensor", fb, [m_bout] if tt == 0 and dc == 0 else [], "t", 1)
                    def th_epi(mil, tt=tt):
                        c = tt % 2
                        w = [m_op[(tt, 3)], mil, m_x[tt]]
                        if tt >= 2 and m_x1f_w[tt - 2] is not None:
                            w.append(m_x1f_w[tt - 2])
                            w.append(m_x1bf_w[tt - 2])
                        if debug and tt == 2:
                            w.append(m_dbg.get("res0"))
                            w.append(m_dbg.get("xs0"))
                        def f1(e, tt=tt, c=c):
                            return e.tensor_add(resid[c][:], out_pp[c][:], x_sb[c][:])
                        m_resid[tt] = em("vector", f1, w, "v", 1)
                        if debug and tt == 0:
                            m_dbg["res0"] = dma("gpsimd", dbg["d_res0"], resid[0][:],
                                                [m_resid[0]])
                            m_dbg["xs0"] = dma("gpsimd", dbg["d_xs0"], x_sb[0][:],
                                               [m_resid[0]])
                        def f2(e, c=c):
                            return e.reduce_sum(sm["sum_x"][:, c:c + 1],
                                                resid[c][:], axis=AX.X)
                        em("vector", f2, [], "v", 1)
                        def f3(e, c=c):
                            return e.activation(x1f[c][:], resid[c][:], AF.Square,
                                                accum_out=sm["ssq"][:, c:c + 1])
                        m_sq = em("scalar", f3, [m_resid[tt]], "a", 1)
                        def f4(e, c=c):
                            return e.tensor_scalar(sm["mvec"][:, c:c + 1],
                                                   sm["sum_x"][:, c:c + 1],
                                                   1.0 / D, None, ALU.mult)
                        em("vector", f4, [], "v", 1)
                        def f5(e, c=c):
                            return e.tensor_mul(sm["msq"][:, c:c + 1],
                                                sm["mvec"][:, c:c + 1],
                                                sm["mvec"][:, c:c + 1])
                        em("vector", f5, [], "v", 1)
                        def f6(e, c=c):
                            return e.tensor_scalar(sm["t1v"][:, c:c + 1],
                                                   sm["ssq"][:, c:c + 1],
                                                   1.0 / D, None, ALU.mult)
                        em("vector", f6, [m_sq], "v", 1)
                        def f7(e, c=c):
                            return e.scalar_tensor_tensor(sm["varv"][:, c:c + 1],
                                                          sm["t1v"][:, c:c + 1],
                                                          EPS, sm["msq"][:, c:c + 1],
                                                          ALU.add, ALU.subtract)
                        m_var = em("vector", f7, [], "v", 1)
                        def f8(e, c=c):
                            return e.activation(sm["lnv"][:, c:c + 1],
                                                sm["varv"][:, c:c + 1], AF.Ln)
                        em("scalar", f8, [m_var], "a", 1)
                        def f9(e, c=c):
                            return e.activation(sm["rstd"][:, c:c + 1],
                                                sm["lnv"][:, c:c + 1],
                                                AF.Exp, scale=-0.5)
                        m_rstd = em("scalar", f9, [], "a", 1)
                        def f10(e, c=c):
                            return e.tensor_scalar(x1f[c][:], resid[c][:],
                                                   sm["mvec"][:, c:c + 1],
                                                   sm["rstd"][:, c:c + 1],
                                                   ALU.subtract, ALU.mult)
                        m_norm[tt] = em("vector", f10, [m_rstd], "v", 1)
                        def f11(e, c=c):
                            return e.activation(x1bf[c][:], x1f[c][:], AF.Identity)
                        m_cast = em("scalar", f11, [m_norm[tt]], "a", 1)
                        m_x1f_w[tt] = dma("sync", x1f_scr[tt * 128:(tt + 1) * 128, :],
                                          x1f[c][:], [m_norm[tt]])
                        m_x1bf_w[tt] = dma("sync", x1bf_scr[tt * 128:(tt + 1) * 128, :],
                                           x1bf[c][:], [m_cast])
                        if tt + 2 < NT:
                            m_x[tt + 2] = dma("sync", x_sb[c][:],
                                              x_d[(tt + 2) * 128:(tt + 3) * 128, :],
                                              [m_resid[tt]])
                    defer_te(th_epi)
                barrier()
                flush()

        with ExitStack() as pc:
            atile = [SB(pc, f"atile{i}", [128, 8 * 128], BF16) for i in range(8)]
            m_at = {}
            with ExitStack() as pc1:
                x1T = SB(pc1, "x1T", [128, NCH * S], BF16)
                sg_sb = SB(pc1, "sg_sb", [128, 8 * S], BF16)
                wsl = [SB(pc1, f"wsl{i}", [128, NCH * NI], BF16) for i in range(2)]
                ringb = SB(pc1, "ringb", [128, 8 * 512], BF16)
                ring = [ringb[:, r * 512:(r + 1) * 512] for r in range(8)]
                identb = SB(pc1, "identb_sb", [128, 128], BF16)
                xrow = [SB(pc1, f"xrow{i}", [128, D], BF16) for i in range(2)]
                lg_pp = [PS(pc1, f"lg_pp{i}", [128, 512], F32) for i in range(2)]
                gu_pp = [PS(pc1, f"gu_pp{i}", [128, 512], F32) for i in range(4)]
                tpool = [PS(pc1, f"tpool{i}", [128, 512], F32) for i in range(2)]

                if debug:
                    dma("gpsimd", dbg["d_x1f"], x1f_scr[:, :], [m_x1f_w[NT - 1]])
                    dma("gpsimd", dbg["d_x1bf"], x1bf_scr[:, :], [m_x1bf_w[NT - 1]])
                m_wg = [None] * NE
                m_wu = [None] * NE
                m_wg0_ch = [None] * NCH
                m_wu0_ch = [None] * NCH
                m_idb = dma("sync", identb[:], identb_d)
                m_xrow = {}
                m_x1T_ev = {}
                tp_war = [None, None]
                tp_te_last = {}
                x1Tv = x1T.rearrange("p (c s) -> p c s", s=S)

                def emit_xrow(tt):
                    w = []
                    if tt >= 2:
                        w.append(tp_te_last[tt - 2])
                    m_xrow[tt] = dma("sync" if tt % 2 == 0 else "scalar",
                                     xrow[tt % 2][:],
                                     x1bf_scr[tt * 128:(tt + 1) * 128, :], w)

                emit_xrow(0)
                emit_xrow(1)
                for tt in range(NT):
                    for grp in range(4):
                        slot = (tt * 4 + grp) % 2
                        mm = None
                        for kk in range(4):
                            ch = grp * 4 + kk
                            w = []
                            if kk == 0:
                                w.append(tp_war[slot])
                                if grp == 0:
                                    w.append(m_xrow[tt])
                                    if tt == 0:
                                        w.append(m_idb)
                            def ftr(e, tt=tt, ch=ch, kk=kk, slot=slot):
                                return e.matmul(tpool[slot][:, kk * 128:(kk + 1) * 128],
                                                xrow[tt % 2][:, ch * 128:(ch + 1) * 128],
                                                identb[:], start=True, stop=True)
                            mm = em("tensor", ftr, w, "t", 1)
                        tp_te_last[tt] = mm
                        lt = ("t", mm[1] + 1)
                        dst = x1Tv[:, grp * 4:(grp + 1) * 4, tt * 128:(tt + 1) * 128]
                        if grp % 2 == 0:
                            def fe(e, dst=dst, slot=slot):
                                return e.activation(
                                    dst, tpool[slot][:].rearrange("p (c k) -> p c k", k=128),
                                    AF.Identity)
                            mev = em("scalar", fe, [lt], "a", 1)
                        else:
                            def fe(e, dst=dst, slot=slot):
                                return e.tensor_copy(
                                    dst, tpool[slot][:].rearrange("p (c k) -> p c k", k=128))
                            mev = em("vector", fe, [lt], "v", 1)
                        m_x1T_ev[(tt, grp)] = mev
                        tp_war[slot] = mev
                    if tt + 2 < NT:
                        emit_xrow(tt + 2)
                    m_wg0_ch[tt] = dma("sync" if tt % 2 == 0 else "scalar",
                                       wsl[0][:, tt * NI:(tt + 1) * NI],
                                       wg0_scr[:, tt * NI:(tt + 1) * NI])
                em("tensor", lambda e: e.drain(), [], "t", 1)
                for c in range(NCH):
                    m_wu0_ch[c] = dma("sync" if c % 2 == 0 else "scalar",
                                      wsl[1][:, c * NI:(c + 1) * NI],
                                      wu0_scr[:, c * NI:(c + 1) * NI])
                m_wg[0] = m_wg0_ch[-1]
                m_wu[0] = m_wu0_ch[-1]

                m_lg_ev = [None] * NT

                def emit_logits(tt):
                    mm = None
                    w = []
                    if tt >= 2:
                        w.append(m_lg_ev[tt - 2])
                    for ch in range(NCH):
                        wc = list(w) if ch == 0 else []
                        if tt == 0:
                            wc.append(m_gwt)
                        def f(e, tt=tt, ch=ch):
                            return e.matmul(lg_pp[tt % 2][:, 0:8],
                                            x1T[:, ch * S + tt * 128: ch * S + tt * 128 + 128],
                                            gwt[:, ch * 8:(ch + 1) * 8],
                                            start=(ch == 0), stop=(ch == NCH - 1))
                        mm = em("tensor", f, wc if wc or ch == 0 else [],
                                "t" if ch in (0, NCH - 1) else None, 1)
                    def th_cmb(mil, tt=tt, mm=mm):
                        def fx(e, tt=tt):
                            return e.activation(el_all[:, tt * 8:(tt + 1) * 8],
                                                lg_pp[tt % 2][:, 0:8], AF.Exp)
                        m_lg_ev[tt] = em("scalar", fx, [mm, mil], "a", 1)
                        el = el_all[:, tt * 8:(tt + 1) * 8]
                        em("vector", lambda e, el=el: e.reduce_max(m1t[:], el, axis=AX.X),
                           [m_lg_ev[tt]], "v", 1)
                        em("vector", lambda e, el=el: e.tensor_scalar(
                            t8a[:], el, m1t[:, 0:1], None, ALU.is_equal), [], "v", 1)
                        em("vector", lambda e, el=el: e.scalar_tensor_tensor(
                            t8b[:], t8a[:], -1e30, el, ALU.mult, ALU.add), [], "v", 1)
                        em("vector", lambda e: e.reduce_max(m2t[:], t8b[:], axis=AX.X),
                           [], "v", 1)
                        em("vector", lambda e, el=el: e.tensor_scalar(
                            t8a[:], el, m2t[:, 0:1], None, ALU.is_ge), [], "v", 1)
                        em("vector", lambda e, el=el: e.tensor_mul(t8b[:], el, t8a[:]),
                           [], "v", 1)
                        em("vector", lambda e: e.tensor_add(s12t[:], m1t[:], m2t[:]),
                           [], "v", 1)
                        em("vector", lambda e: e.reciprocal(rsv[:], s12t[:]), [], "v", 1)
                        em("vector", lambda e, tt=tt: e.tensor_scalar(
                            c_all[:, tt * 8:(tt + 1) * 8], t8b[:], rsv[:, 0:1],
                            None, ALU.mult), [], "v", 1)
                    defer_te(th_cmb)

                m_gu_ev = {}
                m_am = {}
                m_aw = {}
                m_gpass_done = [None] * NE
                m_upass_done = [None] * NE
                gu_jc = [0]

                def emit_gu_pass(e_i, is_u):
                    for ic in range(8):
                        for tc in range(4):
                            j = gu_jc[0]
                            gu_jc[0] += 1
                            slot = wsl[is_u]
                            m_wready = (m_wu if is_u else m_wg)[e_i]
                            w = [m_wready] if (ic == 0 and tc == 0 and e_i > 0) else []
                            if j >= 4:
                                w.append(m_gu_ev[j - 4])
                            mm = None
                            for ch in range(NCH):
                                wch = list(w) if ch == 0 else []
                                if e_i == 0 and ic == 0:
                                    wch.append((m_wu0_ch if is_u else m_wg0_ch)[ch])
                                    if not is_u:
                                        for i4 in range(4):
                                            wch.append(m_x1T_ev[(4 * tc + i4, ch // 4)])
                                def f(e, j=j, ic=ic, tc=tc, ch=ch, slot=slot):
                                    return e.matmul(
                                        gu_pp[j % 4][:],
                                        slot[:, ch * NI + ic * 128: ch * NI + (ic + 1) * 128],
                                        x1T[:, ch * S + tc * 512: ch * S + (tc + 1) * 512],
                                        start=(ch == 0), stop=(ch == NCH - 1))
                                mm = em("tensor", f, wch,
                                        "t" if ch in (0, NCH - 1) else None, 1)
                            if not is_u:
                                wv = [mm]
                                if e_i > 0:
                                    wv.append(m_am.get((e_i - 1, ic, tc)))
                                def fs(e, j=j, ic=ic, tc=tc):
                                    return e.activation(
                                        sg_sb[:, ic * S + tc * 512: ic * S + (tc + 1) * 512],
                                        gu_pp[j % 4][:], AF.Silu)
                                def th_sg(mil, fs=fs, wv=wv, j=j):
                                    m_gu_ev[j] = em("scalar", fs, wv + [mil], "a", 1)
                                defer_te(th_sg)
                            else:
                                r = (ic * 4 + tc) % 8
                                wv = [mm]
                                prev = m_aw.get((e_i, ic * 4 + tc - 8)) if ic * 4 + tc >= 8 \
                                    else (m_aw.get((e_i - 1, 24 + ic * 4 + tc)) if e_i > 0 else None)
                                if prev is not None:
                                    wv.append(prev)
                                def fa(e, j=j, ic=ic, tc=tc, r=r):
                                    return e.tensor_mul(
                                        ring[r][:],
                                        sg_sb[:, ic * S + tc * 512: ic * S + (tc + 1) * 512],
                                        gu_pp[j % 4][:])
                                def th_am(mil, fa=fa, wv=wv, j=j, e_i=e_i,
                                          ic=ic, tc=tc, r=r):
                                    m_am[(e_i, ic, tc)] = em("vector", fa,
                                                             wv + [mil], "v", 1)
                                    m_gu_ev[j] = m_am[(e_i, ic, tc)]
                                    m_aw[(e_i, ic * 4 + tc)] = dma(
                                        "sync",
                                        ats_scr[e_i, ic, :, tc * 512:(tc + 1) * 512],
                                        ring[r][:], [m_am[(e_i, ic, tc)]])
                                defer_te(th_am)
                    if not is_u:
                        m_gpass_done[e_i] = mm
                    else:
                        m_upass_done[e_i] = mm

                emit_gu_pass(0, 0)
                if debug:
                    dma("gpsimd", dbg["d_x1T"], x1T[:], [m_gpass_done[0]])
                for tt in range(NT):
                    emit_logits(tt)
                emit_gu_pass(0, 1)
                m_wg[1] = dma("gpsimd", wsl[0][:], wg_d[1], [m_gpass_done[0]])
                m_wu[1] = dma("gpsimd", wsl[1][:], wu_d[1], [m_upass_done[0]])
                for e_i in range(1, NE):
                    emit_gu_pass(e_i, 0)
                    emit_gu_pass(e_i, 1)
                    if e_i + 1 < NE:
                        m_wg[e_i + 1] = dma("gpsimd", wsl[0][:], wg_d[e_i + 1],
                                            [m_gpass_done[e_i]])
                        m_wu[e_i + 1] = dma("gpsimd", wsl[1][:], wu_d[e_i + 1],
                                            [m_upass_done[e_i]])
                aw_last = m_aw[(0, 31)]
                for k in range(8):
                    m_at[(0, k)] = dma(
                        "sync" if k % 2 == 0 else "scalar",
                        atile[k][:].rearrange("p (ic t) -> p ic t", t=128),
                        ats_scr[0, :, :, k * 128:(k + 1) * 128].rearrange(
                            "ic p t -> p ic t"), [aw_last])
                barrier()
                flush()

            with ExitStack() as pc2:
                acc = SB(pc2, "acc", [128, NT * D], BF16)
                wdsl = [SB(pc2, f"wdsl{i}", [128, 8 * D], BF16) for i in range(2)]
                x1fr = [SB(pc2, f"x1fr{i}", [128, D], F32) for i in range(2)]
                resid2 = SB(pc2, "resid2", [128, D], F32)
                out_sb = [SB(pc2, f"out_sb{i}", [128, D], F32) for i in range(2)]
                y_pp = [PS(pc2, f"y_pp{i}", [128, D], F32) for i in range(2)]

                if debug:
                    dma("gpsimd", dbg["d_el"], el_all[:])
                    dma("gpsimd", dbg["d_call"], c_all[:])
                    dma("gpsimd", dbg["d_at0"], ats_scr[0])
                m_wd = [None] * NE
                m_wd0_ic = [dma("sync" if ic % 2 == 0 else "scalar",
                                wdsl[0][:, ic * D:(ic + 1) * D],
                                wd0_scr[:, ic * D:(ic + 1) * D]) for ic in range(8)]
                m_wd[0] = m_wd0_ic[-1]
                m_wd[1] = dma("gpsimd", wdsl[1][:], wd_d[1])
                m_ymm = {}
                m_stt = {}
                m_x1r = [None] * NT
                m_outw = [None] * NT
                m_outw2 = [None] * NT
                m_ln2 = [None] * NT
                for k in range(2):
                    m_x1r[k] = dma("sync", x1fr[k][:], x1f_scr[k * 128:(k + 1) * 128, :])
                for e_i in range(NE):
                    for tt in range(NT):
                        w = [m_at[(e_i, tt)], m_wd[e_i]]
                        mm = None
                        for dc in range(4):
                            wdc = list(w) if dc == 0 else []
                            if dc == 0:
                                prev = m_stt.get((e_i - 1, tt)) if e_i > 0 else None
                                pv = m_stt.get((e_i, tt - 2)) or \
                                    (m_stt.get((e_i - 1, NT - 2 + tt)) if e_i > 0 else None)
                                if pv is not None:
                                    wdc.append(pv)
                                if prev is not None:
                                    wdc.append(prev)
                            for ic in range(8):
                                wic = wdc if ic == 0 else []
                                if e_i == 0 and tt == 0 and dc == 0:
                                    wic = list(wic) + [m_wd0_ic[ic]]
                                    if ic == 0:
                                        wic.remove(m_wd[0])
                                def f(e, e_i=e_i, tt=tt, dc=dc, ic=ic):
                                    return e.matmul(
                                        y_pp[tt % 2][:, dc * 512:(dc + 1) * 512],
                                        atile[tt % 8][:, ic * 128:(ic + 1) * 128],
                                        wdsl[e_i % 2][:, ic * D + dc * 512: ic * D + (dc + 1) * 512],
                                        start=(ic == 0), stop=(ic == 7))
                                mm = em("tensor", f, wic,
                                        "t" if ic in (0, 7) else None, 1)
                            m_ymm[(e_i, tt, dc)] = mm
                            cs = c_all[:, tt * 8 + e_i: tt * 8 + e_i + 1]
                            dst = acc[:, tt * D + dc * 512: tt * D + (dc + 1) * 512]
                            if e_i == 0:
                                def facc(e, dst=dst, cs=cs, tt=tt, dc=dc):
                                    return e.tensor_scalar(
                                        dst, y_pp[tt % 2][:, dc * 512:(dc + 1) * 512],
                                        cs, None, ALU.mult)
                            else:
                                def facc(e, dst=dst, cs=cs, tt=tt, dc=dc):
                                    return e.scalar_tensor_tensor(
                                        dst, y_pp[tt % 2][:, dc * 512:(dc + 1) * 512],
                                        cs, dst, ALU.mult, ALU.add)

                            def th_stt(mil, facc=facc, mm=mm, e_i=e_i, tt=tt, dc=dc):
                                m_stt[(e_i, tt)] = em("vector", facc, [mm, mil], "v", 1)
                                if not (dc == 3 and e_i == NE - 1):
                                    return
                                c = tt % 2
                                w3 = [m_stt[(e_i, tt)], m_x1r[tt]]
                                def g1(e, tt=tt, c=c):
                                    return e.tensor_add(resid2[:],
                                                        acc[:, tt * D:(tt + 1) * D],
                                                        x1fr[c][:])
                                mr2 = em("vector", g1, w3, "v", 1)
                                def g2(e, c=c):
                                    return e.reduce_sum(sm["sum_x"][:, c:c + 1],
                                                        resid2[:], axis=AX.X)
                                em("vector", g2, [], "v", 1)
                                wsq = [mr2]
                                if tt >= 2 and m_outw[tt - 2] is not None:
                                    wsq.append(m_outw[tt - 2])
                                    wsq.append(m_outw2[tt - 2])
                                def g3(e, c=c):
                                    return e.activation(out_sb[c][:], resid2[:],
                                                        AF.Square,
                                                        accum_out=sm["ssq"][:, c:c + 1])
                                m_sq = em("scalar", g3, wsq, "a", 1)
                                def g4(e, c=c):
                                    return e.tensor_scalar(sm["mvec"][:, c:c + 1],
                                                           sm["sum_x"][:, c:c + 1],
                                                           1.0 / D, None, ALU.mult)
                                em("vector", g4, [], "v", 1)
                                def g5(e, c=c):
                                    return e.tensor_mul(sm["msq"][:, c:c + 1],
                                                        sm["mvec"][:, c:c + 1],
                                                        sm["mvec"][:, c:c + 1])
                                em("vector", g5, [], "v", 1)
                                def g6(e, c=c):
                                    return e.tensor_scalar(sm["t1v"][:, c:c + 1],
                                                           sm["ssq"][:, c:c + 1],
                                                           1.0 / D, None, ALU.mult)
                                em("vector", g6, [m_sq], "v", 1)
                                def g7(e, c=c):
                                    return e.scalar_tensor_tensor(
                                        sm["varv"][:, c:c + 1], sm["t1v"][:, c:c + 1],
                                        EPS, sm["msq"][:, c:c + 1],
                                        ALU.add, ALU.subtract)
                                m_var = em("vector", g7, [], "v", 1)
                                def g8(e, c=c):
                                    return e.activation(sm["lnv"][:, c:c + 1],
                                                        sm["varv"][:, c:c + 1], AF.Ln)
                                em("scalar", g8, [m_var], "a", 1)
                                def g9(e, c=c):
                                    return e.activation(sm["rstd"][:, c:c + 1],
                                                        sm["lnv"][:, c:c + 1],
                                                        AF.Exp, scale=-0.5)
                                m_rstd = em("scalar", g9, [], "a", 1)
                                def g10(e, c=c):
                                    return e.tensor_scalar(out_sb[c][:], resid2[:],
                                                           sm["mvec"][:, c:c + 1],
                                                           sm["rstd"][:, c:c + 1],
                                                           ALU.subtract, ALU.mult)
                                m_ln2[tt] = em("vector", g10, [m_rstd], "v", 1)
                                def gout(e, tt=tt, c=c):
                                    return e.dma_start(out_d[tt * 128:(tt + 1) * 128,
                                                             0:D // 2],
                                                       out_sb[c][:, 0:D // 2])
                                mo1 = em("gpsimd", gout, [m_ln2[tt]], "do", 16)
                                mo2 = dma("scalar",
                                          out_d[tt * 128:(tt + 1) * 128, D // 2:D],
                                          out_sb[c][:, D // 2:D], [m_ln2[tt]])
                                m_outw[tt] = mo1
                                m_outw2[tt] = mo2
                                if tt + 2 < NT:
                                    m_x1r[tt + 2] = dma(
                                        "sync", x1fr[c][:],
                                        x1f_scr[(tt + 2) * 128:(tt + 3) * 128, :],
                                        [mr2])
                            defer_te(th_stt)
                        ne_, nt_ = (e_i, tt + 8) if tt + 8 < NT else \
                            ((e_i + 1, tt + 8 - NT) if e_i + 1 < NE else (None, None))
                        if ne_ is not None:
                            m_at[(ne_, nt_)] = dma(
                                "sync" if nt_ % 2 == 0 else "scalar",
                                atile[tt % 8][:].rearrange("p (ic t) -> p ic t", t=128),
                                ats_scr[ne_, :, :, nt_ * 128:(nt_ + 1) * 128].rearrange(
                                    "ic p t -> p ic t"),
                                [m_ymm[(e_i, tt, 3)]])
                    if e_i + 2 < NE:
                        m_wd[e_i + 2] = dma("gpsimd", wdsl[e_i % 2][:],
                                            wd_d[e_i + 2],
                                            [m_ymm[(e_i, NT - 1, 3)]])
                    if e_i == 0:
                        pass
                te_tail()
                if debug:
                    dma("gpsimd", dbg["d_acc"], acc[:], [m_stt[(NE - 1, NT - 1)]])
                em("gpsimd", None, [("do", cnt["do"]), ("dg", cnt["dg"]),
                                    ("ds", cnt["ds"]), ("dc", cnt.get("dc", 0))])
                flush()
    return nc


_NC_CACHE = None


def kernel(L0_attn_x, L0_attn_W_qkv, L0_attn_b_qkv, L0_attn_W_out, L0_attn_b_out,
           L0_ffn_gate_w, L0_ffn_w_gate, L0_ffn_w_up, L0_ffn_w_down):
    global _NC_CACHE
    f32 = np.float32
    x = np.asarray(L0_attn_x, f32)
    wqkv = np.asarray(L0_attn_W_qkv, f32)
    bqkv = np.asarray(L0_attn_b_qkv, f32)
    wout = np.asarray(L0_attn_W_out, f32)
    bout = np.asarray(L0_attn_b_out, f32)
    gw = np.asarray(L0_ffn_gate_w, f32)
    wg = np.asarray(L0_ffn_w_gate, f32)
    wu = np.asarray(L0_ffn_w_up, f32)
    wd = np.asarray(L0_ffn_w_down, f32)

    wqkvt = np.ascontiguousarray(
        wqkv.reshape(16, 128, 16, 384).transpose(2, 1, 0, 3)).reshape(16, 128, 16 * 384)
    bqkvt = np.ascontiguousarray(bqkv.reshape(48, 128).T)
    woutt = np.ascontiguousarray(
        wout.reshape(16, 128, 2048).transpose(1, 0, 2)).reshape(128, 16 * 2048)
    boutr = bout.reshape(1, 2048)
    gwt = np.ascontiguousarray(gw.T.reshape(16, 128, 8).transpose(1, 0, 2)).reshape(128, 128)
    wgt = np.ascontiguousarray(
        wg.reshape(8, 16, 128, 1024).transpose(0, 2, 1, 3)).reshape(8, 128, 16 * 1024)
    wut = np.ascontiguousarray(
        wu.reshape(8, 16, 128, 1024).transpose(0, 2, 1, 3)).reshape(8, 128, 16 * 1024)
    wdt = np.ascontiguousarray(
        wd.reshape(8, 8, 128, 2048).transpose(0, 2, 1, 3)).reshape(8, 128, 8 * 2048)

    kp = np.arange(128)[:, None]
    qf = np.arange(512)[None, :]
    masks = np.concatenate([(qf >= kp + 128 * r) for r in range(4)], axis=1)
    masks = masks.astype(np.float32).astype(BF)
    ones_col = np.ones((128, 1), np.float32).astype(BF)
    ones_row = np.ones((1, 512), np.float32).astype(BF)

    if _NC_CACHE is None:
        _NC_CACHE = build_nc()
    nc = _NC_CACHE

    common = {
        "wqkv": wqkvt, "bqkvt": bqkvt, "woutt": woutt, "bout": boutr,
        "gwt": gwt, "wgt": wgt, "wut": wut, "wdt": wdt,
        "masks": masks, "ones_col": ones_col, "ones_row": ones_row,
        "ones_sq": np.ones((128, 128), np.float32).astype(BF),
        "ident": np.eye(128, dtype=np.float32),
        "identb": np.eye(128, dtype=np.float32).astype(BF),
    }
    in_maps = [dict(common, x=np.ascontiguousarray(x[b])) for b in range(8)]
    res = run_bass_kernel_spmd(nc, in_maps, core_ids=list(range(8)))
    return np.stack([res.results[b]["out"] for b in range(8)], axis=0)


# version 20
# speedup vs baseline: 1.0580x; 1.0580x over previous
import numpy as np
import ml_dtypes
import concourse.bass as bass
import concourse.mybir as mybir
from concourse.bass_utils import run_bass_kernel_spmd
from contextlib import ExitStack

F32 = mybir.dt.float32
BF16 = mybir.dt.bfloat16
AF = mybir.ActivationFunctionType
ALU = mybir.AluOpType
AX = mybir.AxisListType
BF = ml_dtypes.bfloat16

S = 2048
D = 2048
NH = 16
HD = 128
NI = 1024
NE = 8
NT = 16
NCH = 16
SCALE = 0.08838834764831845
EPS = 1e-5

DBG_SKIP_NORM = False

ENGINES = ("sync", "gpsimd", "tensor", "vector", "scalar")
OWNER = {"t": "tensor", "a": "scalar", "v": "vector"}


def build_nc(debug=False):
    nc = bass.Bass("TRN2", target_bir_lowering=False)

    def din(name, shape, dt=F32):
        return nc.dram_tensor(name, shape, dt, kind="ExternalInput").ap()
    x_d = din("x", [S, D])
    wqkv_d = din("wqkv", [NH, 128, 16 * 384])
    bqkv_d = din("bqkvt", [128, 48])
    wout_d = din("woutt", [128, NCH * D])
    bout_d = din("bout", [1, D])
    gwt_d = din("gwt", [128, 128])
    wg_d = din("wgt", [NE, 128, NCH * NI])
    wu_d = din("wut", [NE, 128, NCH * NI])
    wd_d = din("wdt", [NE, 128, 8 * D])
    masks_d = din("masks", [128, 4 * 512], BF16)
    onesc_d = din("ones_col", [128, 1], BF16)
    onesr_d = din("ones_row", [1, 512], BF16)
    onessq_d = din("ones_sq", [128, 128], BF16)
    out_d = nc.dram_tensor("out", [S, D], F32, kind="ExternalOutput").ap()
    dbg = {}
    if debug:
        for nm, sh, dt in [("d_x1T", [128, NCH * S], BF16),
                           ("d_xT", [128, NCH * S], BF16),
                           ("d_qT", [128, S], BF16), ("d_kT", [128, S], BF16),
                           ("d_vT", [128, S], BF16), ("d_v0", [128, S], BF16),
                           ("d_ctx", [128, NH * S], BF16),
                           ("d_x1f", [S, D], F32), ("d_x1bf", [S, D], BF16),
                           ("d_el", [128, NT * 8], F32), ("d_call", [128, NT * 8], F32),
                           ("d_at0", [8, 128, S], BF16),
                           ("d_res0", [128, D], F32), ("d_xs0", [128, D], F32),
                           ("d_acc", [128, NT * D], BF16)]:
            dbg[nm] = nc.dram_tensor(nm, sh, dt, kind="ExternalOutput").ap()

    ident_d = din("ident", [128, 128])
    identb_d = din("identb", [128, 128], BF16)
    vt_scr = nc.dram_tensor("vt_scr", [2, 128, S], BF16, kind="Internal").ap()
    x1f_scr = nc.dram_tensor("x1f_scr", [S, D], F32, kind="Internal").ap()
    x1bf_scr = nc.dram_tensor("x1bf_scr", [S, D], BF16, kind="Internal").ap()
    ats_scr = nc.dram_tensor("ats_scr", [NE, 8, 128, S], BF16, kind="Internal").ap()
    wout_scr = nc.dram_tensor("wout_scr", [128, NCH * D], BF16, kind="Internal").ap()
    wg0_scr = nc.dram_tensor("wg0_scr", [128, NCH * NI], BF16, kind="Internal").ap()
    wu0_scr = nc.dram_tensor("wu0_scr", [128, NCH * NI], BF16, kind="Internal").ap()
    wd0_scr = nc.dram_tensor("wd0_scr", [128, 8 * D], BF16, kind="Internal").ap()

    cnt = {}
    water = {e: {} for e in ENGINES}
    ops = []
    sems = {}

    pend_after_te = []

    dummy_flush = {}

    def em(eng, fn, waits=(), sem=None, inc=1):
        ws = []
        for w in waits:
            if w is None:
                continue
            sname, val = w
            if OWNER.get(sname) == eng:
                continue
            ws.append((sname, val))
        mil = None
        if sem in ("a", "v") and fn is not None:
            ops.append((eng, fn, None, 0, ws))
            fn = dummy_flush[eng]
            ws = []
        sem = {"ar": "a", "vr": "v"}.get(sem, sem)
        if sem is not None:
            cnt[sem] = cnt.get(sem, 0) + inc
            mil = (sem, cnt[sem])
        ops.append((eng, fn, sem, inc if sem is not None else 0, ws))
        if eng == "tensor" and sem == "t" and pend_after_te:
            thunks = list(pend_after_te)
            pend_after_te.clear()
            for th in thunks:
                th(mil)
        return mil

    def defer_te(thunk):
        pend_after_te.append(thunk)

    def te_tail():
        if pend_after_te:
            em("tensor", lambda e: e.drain(), [], "t", 1)

    def dma(eng, out, in_, waits=(), transpose=False):
        sname = {"sync": "ds", "gpsimd": "dg", "scalar": "dc"}[eng]
        if transpose:
            fn = lambda e, o=out, i=in_: e.dma_start_transpose(o, i)
        else:
            fn = lambda e, o=out, i=in_: e.dma_start(o, i)
        return em(eng, fn, waits, sname, 16)

    def barrier():
        te_tail()
        for e in ENGINES:
            w = []
            if e == "sync":
                w = [("ds", cnt.get("ds", 0))]
            if e == "gpsimd":
                w = [("dg", cnt.get("dg", 0)), ("do", cnt.get("do", 0))]
            if e == "scalar":
                w = [("dc", cnt.get("dc", 0))]
            em(e, lambda eng: eng.nop(), w, "bar", 1)
        target = cnt["bar"]
        for e in ENGINES:
            em(e, None, [("bar", target)])

    def flush():
        with nc.Block() as blk:
            for en in ENGINES:
                def make(en):
                    def run(e):
                        wm = water[en]
                        for (oeng, fn, sem, inc, ws) in ops:
                            if oeng != en:
                                continue
                            for (sname, val) in ws:
                                if wm.get(sname, 0) >= val:
                                    continue
                                e.wait_ge(sems[sname], val)
                                wm[sname] = val
                            if fn is None:
                                continue
                            ins = fn(e)
                            if sem is not None:
                                ins.then_inc(sems[sem], inc)
                    return run
                getattr(blk, en)(make(en))
        ops.clear()

    with ExitStack() as g:
        for sname in ("bar", "ds", "dg", "dc", "do", "t", "a", "v"):
            sems[sname] = g.enter_context(nc.semaphore(sname))
        SB = lambda st, *a: st.enter_context(nc.sbuf_tensor(*a))
        PS = lambda st, *a: st.enter_context(nc.psum_tensor(*a))

        smf = SB(g, "smallf", [128, 344], F32)
        _fo = [0]
        def carvef(n):
            a = _fo[0]
            _fo[0] += n
            return smf[:, a:a + n]
        dmv = carvef(2)
        dma_t = carvef(2)
        bqkvt = carvef(48)
        c_all = carvef(NT * 8)
        el_all = carvef(NT * 8)
        sm = {k: carvef(2)
              for k in ("sum_x", "ssq", "mvec", "msq", "t1v", "varv", "lnv", "rstd")}
        t8a = carvef(8)
        t8b = carvef(8)
        m1t = carvef(1)
        m2t = carvef(1)
        s12t = carvef(1)
        rsv = carvef(1)
        smb = SB(g, "smallb", [128, 2308], BF16)
        _bo = [0]
        def carveb(n):
            a = _bo[0]
            _bo[0] += n
            return smb[:, a:a + n]
        masks = carveb(4 * 512)
        onesc = carveb(1)
        onessq = carveb(128)
        gwt = carveb(128)
        dummy_flush["vector"] = lambda e: e.tensor_copy(dmv[:, 0:1], dmv[:, 1:2])
        dummy_flush["scalar"] = lambda e: e.copy(dma_t[:, 0:1], dma_t[:, 1:2])
        onesr = SB(g, "onesr_sb", [1, 512], BF16)
        bout = SB(g, "bout_sb", [1, D], BF16)

        with ExitStack() as pab:
            ctx_all = SB(pab, "ctx_all", [128, NH * S], BF16)
            with ExitStack() as pa:
                xT = SB(pa, "xT", [128, NCH * S], BF16)
                xfs = [SB(pa, f"xfs{i}", [128, D], F32) for i in range(2)]
                identsb = SB(pa, "identsb", [128, 128], F32)
                wbuf = [SB(pa, f"wbuf{i}", [128, 16 * 384], BF16) for i in range(2)]
                qT = SB(pa, "qT", [128, S], BF16)
                kT = SB(pa, "kT", [128, S], BF16)
                vT = SB(pa, "vT", [128, S], BF16)
                v_sb = [SB(pa, f"v_sb{i}", [128, S], BF16) for i in range(2)]
                p_sb = [SB(pa, f"p_sb{i}", [128, 512], BF16) for i in range(4)]
                lnden = SB(pa, "lnden", [128, 512], F32)
                rdbc = SB(pa, "rdbc", [128, 512], F32)

                pool = [PS(pa, f"pool{i}", [128, 512], F32) for i in range(4)]
                ctx_pp = [PS(pa, f"ctx_pp{i}", [128, 512], F32) for i in range(2)]
                den_pp = [PS(pa, f"den_pp{i}", [128, 512], F32) for i in range(2)]

                dma("sync", masks[:], masks_d)
                dma("sync", onesc[:], onesc_d)
                dma("sync", onesr[:], onesr_d)
                dma("sync", onessq[:], onessq_d)
                dma("sync", bqkvt[:], bqkv_d)
                m_gwt = dma("gpsimd", gwt[:], gwt_d)
                m_bout = dma("gpsimd", bout[:], bout_d)

                m_w = [None] * NH
                m_w[0] = dma("gpsimd", wbuf[0][:], wqkv_d[0])
                m_w[1] = dma("gpsimd", wbuf[1][:], wqkv_d[1])

                pool_war = [None] * 4
                pool_ctr = [0]
                m_ident = dma("sync", identsb[:], ident_d)
                m_xfr = [None] * NT
                m_xev_last = {}
                q_gate = [None] * 4

                def emit_xread(tt):
                    w = []
                    if tt >= 2:
                        w += list(m_xev_last[tt - 2].values())
                    m_xfr[tt] = dma("sync" if tt % 2 == 0 else "scalar",
                                    xfs[tt % 2][:],
                                    x_d[tt * 128:(tt + 1) * 128, :], w)

                emit_xread(0)
                emit_xread(1)
                for tt in range(NT):
                    m_xev_last[tt] = {}
                    for grp in range(4):
                        slot = pool_ctr[0] % 4
                        pool_ctr[0] += 1
                        mm = None
                        for k in range(4):
                            ch = grp * 4 + k
                            w = [pool_war[slot]] if k == 0 else []
                            if grp == 0 and k == 0:
                                w.append(m_xfr[tt])
                                if tt == 0:
                                    w.append(m_ident)
                            def ftr(e, tt=tt, ch=ch, k=k, slot=slot):
                                return e.matmul(pool[slot][:, k * 128:(k + 1) * 128],
                                                xfs[tt % 2][:, ch * 128:(ch + 1) * 128],
                                                identsb[:], start=True, stop=True)
                            mm = em("tensor", ftr, w, "t", 1)
                        lt = ("t", mm[1] + 1)
                        dst = xT.rearrange("p (c s) -> p c s", s=S)[
                            :, grp * 4:(grp + 1) * 4, tt * 128:(tt + 1) * 128]
                        if grp % 2 == 0:
                            def fe(e, dst=dst, slot=slot):
                                return e.activation(
                                    dst, pool[slot][:].rearrange("p (c k) -> p c k", k=128),
                                    AF.Identity)
                            mev = em("scalar", fe, [lt], "a", 1)
                            m_xev_last[tt]["a"] = mev
                        else:
                            def fe(e, dst=dst, slot=slot):
                                return e.tensor_copy(
                                    dst, pool[slot][:].rearrange("p (c k) -> p c k", k=128))
                            mev = em("vector", fe, [lt], "v", 1)
                            m_xev_last[tt]["v"] = mev
                        pool_war[slot] = mev
                    if tt + 2 < NT:
                        emit_xread(tt + 2)
                em("tensor", lambda e: e.drain(), [], "t", 1)
                for q in range(4):
                    g = []
                    for tt in range(4 * q, 4 * q + 4):
                        g += list(m_xev_last[tt].values())
                    q_gate[q] = g
                m_xT = q_gate[3]

                last_ctx_consumer = [None, None]
                last_den_consumer = [None, None]
                last_rdbc = [None]
                m_vscr_w = [[None] * 4, [None] * 4]
                m_vread = [None, None]
                m_vreadc = [None, None]
                last_ctxread = [None, None]
                p_war = [None] * 4
                m_denlast = {}
                m_ctx_stop = {}
                m_qkv_ev = {}
                m_qkv_done = {}
                sc_j = [0]

                m_dbg = {}
                for h in range(NH):
                    if debug and h == 0:
                        m_dbg["xT"] = dma("gpsimd", dbg["d_xT"], xT[:], m_xT)
                    if debug and h == 1:
                        m_dbg["q"] = dma("gpsimd", dbg["d_qT"], qT[:], [m_qkv_ev[(0, 11)]])
                        m_dbg["k"] = dma("gpsimd", dbg["d_kT"], kT[:], [m_qkv_ev[(0, 11)]])
                        m_dbg["v"] = dma("gpsimd", dbg["d_vT"], vT[:], [m_qkv_ev[(0, 11)]])
                        m_dbg["v0"] = dma("gpsimd", dbg["d_v0"], v_sb[0][:], [m_vread[0]])
                    if 2 <= h <= 5:
                        k = h - 2
                        dma("gpsimd", wout_scr[:, k * 8192:(k + 1) * 8192],
                            wout_d[:, k * 8192:(k + 1) * 8192])
                    if 6 <= h <= 9:
                        for i in range(4):
                            c = 4 * (h - 6) + i
                            dma("gpsimd", wg0_scr[:, c * NI:(c + 1) * NI],
                                wg_d[0][:, c * NI:(c + 1) * NI])
                    if 10 <= h <= 13:
                        for i in range(4):
                            c = 4 * (h - 10) + i
                            dma("gpsimd", wu0_scr[:, c * NI:(c + 1) * NI],
                                wu_d[0][:, c * NI:(c + 1) * NI])
                    grp_m = []
                    for gi in range(12):
                        op = (2, 0, 1)[gi // 4]
                        tc = gi % 4
                        slot = pool_ctr[0] % 4
                        pool_ctr[0] += 1
                        ps = pool[slot]
                        w = [m_w[h], pool_war[slot]]
                        if debug and h == 1 and gi == 0:
                            w += [m_dbg["q"], m_dbg["k"], m_dbg["v"], m_dbg["v0"]]
                        if h == 0 and gi < 4:
                            w += q_gate[tc]
                        mm = None
                        for ch in range(NCH):
                            def f(e, ps=ps, ch=ch, op=op, tc=tc, h=h):
                                return e.matmul(
                                    ps[:],
                                    wbuf[h % 2][:, ch * 384 + op * 128: ch * 384 + (op + 1) * 128],
                                    xT[:, ch * S + tc * 512: ch * S + (tc + 1) * 512],
                                    start=(ch == 0), stop=(ch == NCH - 1))
                            mm = em("tensor", f, w if ch == 0 else [],
                                    "t" if ch in (0, NCH - 1) else None, 1)
                        grp_m.append(mm)
                        dest = (qT, kT, vT)[op]
                        gcol = 3 * h + op
                        def fe(e, ps=ps, dest=dest, tc=tc, gcol=gcol):
                            return e.activation(dest[:, tc * 512:(tc + 1) * 512], ps[:],
                                                AF.Identity, bias=bqkvt[:, gcol:gcol + 1],
                                                scale=1.0)
                        evw = [mm]
                        if op == 2 and h > 0:
                            evw.append(m_vscr_w[(h - 1) % 2][tc])
                        def th_ev(mil, fe=fe, evw=evw, h=h, gi=gi, slot=slot,
                                  op=op, tc=tc):
                            style = "a" if gi == 11 else "ar"
                            mev = em("scalar", fe, evw + [mil], style, 1)
                            m_qkv_ev[(h, gi)] = mev
                            pool_war[slot] = mev
                            m_qkv_done[(h, gi - 1)] = mev
                            if gi == 11:
                                m_qkv_done[(h, 11)] = mev
                            if 1 <= gi <= 4:
                                btc = gi - 1
                                wv = [mev]
                                if m_vread[h % 2] is not None:
                                    wv.append(m_vread[h % 2])
                                    wv.append(m_vreadc[h % 2])
                                m_vscr_w[h % 2][btc] = dma(
                                    "sync", vt_scr[h % 2][:, btc * 512:(btc + 1) * 512],
                                    vT[:, btc * 512:(btc + 1) * 512], wv)
                        defer_te(th_ev)

                    mv_s = None
                    mv_c = None
                    for kt in range(NT):
                        w = [m_vscr_w[h % 2][kt // 4]]
                        if kt < 2 and last_ctxread[h % 2] is not None:
                            w.append(last_ctxread[h % 2])
                        mv_s = dma("sync", v_sb[h % 2][:, kt * 128:(kt + 1) * 128],
                                   vt_scr[h % 2][:, kt * 128:(kt + 1) * 128], w,
                                   transpose=True)
                    m_vread[h % 2] = mv_s
                    m_vreadc[h % 2] = mv_s
                    m_vread2 = mv_s

                    steps = [(qc, kt) for qc in range(4) for kt in range(4 * qc + 4)]
                    NSTEP = len(steps)
                    m_exp = {}
                    m_mask = {}
                    s_slot = {}

                    def is_diag(j):
                        qc, kt = steps[j]
                        return kt >= 4 * qc

                    m_s_t = {}

                    def emit_S(j, h=h, mv_c=m_vread2):
                        qc, kt = steps[j]
                        slot = pool_ctr[0] % 4
                        pool_ctr[0] += 1
                        s_slot[j] = slot
                        w = [m_qkv_done[(h, 4 + qc)], m_qkv_done[(h, 8 + kt // 4)],
                             pool_war[slot]]
                        if kt == 0:
                            if last_ctx_consumer[qc % 2] is not None:
                                w.append(last_ctx_consumer[qc % 2])
                            if last_den_consumer[qc % 2] is not None:
                                w.append(last_den_consumer[qc % 2])
                        if j == 0:
                            w.append(m_vread[h % 2])
                            w.append(mv_c)
                        def fs(e, kt=kt, qc=qc, slot=slot):
                            return e.matmul(pool[slot][:],
                                            kT[:, kt * 128:(kt + 1) * 128],
                                            qT[:, qc * 512:(qc + 1) * 512],
                                            start=True, stop=True)
                        m_s_t[j] = em("tensor", fs, w, "t", 1)

                    def emit_exp(j):
                        lt = ("t", m_s_t[j][1] + 1)
                        slot = s_slot[j]
                        w = [lt, p_war[j % 4]]
                        def fx(e, j=j, slot=slot):
                            return e.activation(p_sb[j % 4][:], pool[slot][:],
                                                AF.Exp, scale=SCALE)
                        style = "a" if is_diag(j) else "ar"
                        m_exp[j] = em("scalar", fx, w, style, 1)
                        pool_war[slot] = m_exp[j]
                        if is_diag(j):
                            emit_mask(j)

                    def emit_mask(j):
                        qc, kt = steps[j]
                        r = kt - 4 * qc
                        def fmk(e, j=j, r=r):
                            return e.tensor_mul(p_sb[j % 4][:], p_sb[j % 4][:],
                                                masks[:, r * 512:(r + 1) * 512])
                        m_mask[j] = em("vector", fmk, [m_exp[j]], "v", 1)

                    def emit_cd(j, h=h):
                        qc, kt = steps[j]
                        nkt = 4 * qc + 4
                        pw = [m_mask[j]] if is_diag(j) else [m_exp[j + 1]]
                        def fc(e, kt=kt, qc=qc, j=j, h=h):
                            return e.matmul(ctx_pp[qc % 2][:],
                                            v_sb[h % 2][:, kt * 128:(kt + 1) * 128],
                                            p_sb[j % 4][:],
                                            start=(kt == 0), stop=(kt == nkt - 1))
                        mc = em("tensor", fc, pw, "t", 1)
                        def fd(e, kt=kt, qc=qc, j=j):
                            return e.matmul(den_pp[qc % 2][:], onessq[:],
                                            p_sb[j % 4][:],
                                            start=(kt == 0), stop=(kt == nkt - 1))
                        md = em("tensor", fd, [], "t", 1)
                        p_war[j % 4] = md
                        if kt == nkt - 1:
                            m_ctx_stop[(h, qc)] = mc
                            m_denlast[(h, qc)] = md
                            last_ctxread[h % 2] = mc
                            def th_epi(mil, h=h, qc=qc):
                                w_rd = [m_denlast[(h, qc)], mil]
                                if last_rdbc[0] is not None:
                                    w_rd.append(last_rdbc[0])
                                def fr(e, qc=qc):
                                    return e.reciprocal(rdbc[:], den_pp[qc % 2][:])
                                m_rd = em("vector", fr, w_rd, "v", 1)
                                last_den_consumer[qc % 2] = m_rd
                                def fm(e, h=h, qc=qc):
                                    return e.tensor_tensor(
                                        ctx_all[:, h * S + qc * 512: h * S + (qc + 1) * 512],
                                        ctx_pp[qc % 2][:], rdbc[:], ALU.mult)
                                mfm = em("vector", fm,
                                         [m_ctx_stop[(h, qc)]], "v", 1)
                                last_ctx_consumer[qc % 2] = mfm
                                last_rdbc[0] = mfm
                            defer_te(th_epi)

                    emit_S(0)
                    emit_S(1)
                    emit_S(2)
                    emit_exp(0)
                    emit_S(3)
                    emit_exp(1)
                    for i in range(NSTEP):
                        emit_cd(i)
                        if i + 4 < NSTEP:
                            emit_S(i + 4)
                        if i + 2 < NSTEP:
                            emit_exp(i + 2)

                    if h + 2 < NH:
                        m_w[h + 2] = dma("gpsimd", wbuf[h % 2][:], wqkv_d[h + 2],
                                         [grp_m[-1]])

                barrier()
                flush()

            with ExitStack() as pb:
                wout_sb = SB(pb, "wout_sb", [128, NCH * D], BF16)
                x_sb = [SB(pb, f"x_sb{i}", [128, D], F32) for i in range(2)]
                resid = [SB(pb, f"resid{i}", [128, D], F32) for i in range(2)]
                x1f = [SB(pb, f"x1f{i}", [128, D], F32) for i in range(2)]
                x1bf = [SB(pb, f"x1bf{i}", [128, D], BF16) for i in range(2)]
                out_pp = [PS(pb, f"out_pp{i}", [128, D], F32) for i in range(2)]

                if debug:
                    dma("gpsimd", dbg["d_ctx"], ctx_all[:])
                for ic in range(8):
                    dma("gpsimd", wd0_scr[:, ic * D:(ic + 1) * D],
                        wd_d[0][:, ic * D:(ic + 1) * D])
                m_wout_k = [None] * 16
                for k in range(16):
                    m_wout_k[k] = dma("sync" if k % 2 == 0 else "scalar",
                                      wout_sb[:, k * 2048:(k + 1) * 2048],
                                      wout_scr[:, k * 2048:(k + 1) * 2048])
                m_x = [None] * NT
                m_x1f_w = [None] * NT
                m_x1bf_w = [None] * NT
                m_resid = [None] * NT
                m_norm = [None] * NT
                m_op = {}
                for tt in range(2):
                    m_x[tt] = dma("sync", x_sb[tt % 2][:],
                                  x_d[tt * 128:(tt + 1) * 128, :])
                for tt in range(NT):
                    for dc in range(4):
                        w = []
                        if dc == 0 and tt >= 2:
                            w.append(m_resid[tt - 2])
                        mm = None
                        for cc in range(NCH):
                            if cc == 0:
                                wc = list(w)
                            else:
                                wc = []
                            if tt == 0:
                                wc.append(m_wout_k[cc])
                            def f(e, tt=tt, dc=dc, cc=cc):
                                return e.matmul(
                                    out_pp[tt % 2][:, dc * 512:(dc + 1) * 512],
                                    ctx_all[:, cc * S + tt * 128: cc * S + tt * 128 + 128],
                                    wout_sb[:, cc * D + dc * 512: cc * D + (dc + 1) * 512],
                                    start=(cc == 0), stop=False)
                            mm = em("tensor", f, wc,
                                    "t" if cc == 0 else None, 1)
                        def fb(e, tt=tt, dc=dc):
                            return e.matmul(out_pp[tt % 2][:, dc * 512:(dc + 1) * 512],
                                            onesr[0:1, 0:128],
                                            bout[0:1, dc * 512:(dc + 1) * 512],
                                            start=False, stop=True)
                        m_op[(tt, dc)] = em("tensor", fb, [m_bout] if tt == 0 and dc == 0 else [], "t", 1)
                    def th_epi(mil, tt=tt):
                        c = tt % 2
                        w = [m_op[(tt, 3)], mil, m_x[tt]]
                        if tt >= 2 and m_x1f_w[tt - 2] is not None:
                            w.append(m_x1f_w[tt - 2])
                            w.append(m_x1bf_w[tt - 2])
                        if debug and tt == 2:
                            w.append(m_dbg.get("res0"))
                            w.append(m_dbg.get("xs0"))
                        def f1(e, tt=tt, c=c):
                            return e.tensor_add(resid[c][:], out_pp[c][:], x_sb[c][:])
                        m_resid[tt] = em("vector", f1, w, "v", 1)
                        if debug and tt == 0:
                            m_dbg["res0"] = dma("gpsimd", dbg["d_res0"], resid[0][:],
                                                [m_resid[0]])
                            m_dbg["xs0"] = dma("gpsimd", dbg["d_xs0"], x_sb[0][:],
                                               [m_resid[0]])
                        def f2(e, c=c):
                            return e.reduce_sum(sm["sum_x"][:, c:c + 1],
                                                resid[c][:], axis=AX.X)
                        em("vector", f2, [], "v", 1)
                        def f3(e, c=c):
                            return e.activation(x1f[c][:], resid[c][:], AF.Square,
                                                accum_out=sm["ssq"][:, c:c + 1])
                        m_sq = em("scalar", f3, [m_resid[tt]], "a", 1)
                        def f4(e, c=c):
                            return e.tensor_scalar(sm["mvec"][:, c:c + 1],
                                                   sm["sum_x"][:, c:c + 1],
                                                   1.0 / D, None, ALU.mult)
                        em("vector", f4, [], "v", 1)
                        def f5(e, c=c):
                            return e.tensor_mul(sm["msq"][:, c:c + 1],
                                                sm["mvec"][:, c:c + 1],
                                                sm["mvec"][:, c:c + 1])
                        em("vector", f5, [], "v", 1)
                        def f6(e, c=c):
                            return e.tensor_scalar(sm["t1v"][:, c:c + 1],
                                                   sm["ssq"][:, c:c + 1],
                                                   1.0 / D, None, ALU.mult)
                        em("vector", f6, [m_sq], "v", 1)
                        def f7(e, c=c):
                            return e.scalar_tensor_tensor(sm["varv"][:, c:c + 1],
                                                          sm["t1v"][:, c:c + 1],
                                                          EPS, sm["msq"][:, c:c + 1],
                                                          ALU.add, ALU.subtract)
                        m_var = em("vector", f7, [], "v", 1)
                        def f8(e, c=c):
                            return e.activation(sm["lnv"][:, c:c + 1],
                                                sm["varv"][:, c:c + 1], AF.Ln)
                        em("scalar", f8, [m_var], "a", 1)
                        def f9(e, c=c):
                            return e.activation(sm["rstd"][:, c:c + 1],
                                                sm["lnv"][:, c:c + 1],
                                                AF.Exp, scale=-0.5)
                        m_rstd = em("scalar", f9, [], "a", 1)
                        def f10(e, c=c):
                            return e.tensor_scalar(x1f[c][:], resid[c][:],
                                                   sm["mvec"][:, c:c + 1],
                                                   sm["rstd"][:, c:c + 1],
                                                   ALU.subtract, ALU.mult)
                        m_norm[tt] = em("vector", f10, [m_rstd], "v", 1)
                        def f11(e, c=c):
                            return e.activation(x1bf[c][:], x1f[c][:], AF.Identity)
                        m_cast = em("scalar", f11, [m_norm[tt]], "a", 1)
                        m_x1f_w[tt] = dma("sync", x1f_scr[tt * 128:(tt + 1) * 128, :],
                                          x1f[c][:], [m_norm[tt]])
                        m_x1bf_w[tt] = dma("sync", x1bf_scr[tt * 128:(tt + 1) * 128, :],
                                           x1bf[c][:], [m_cast])
                        if tt + 2 < NT:
                            m_x[tt + 2] = dma("sync", x_sb[c][:],
                                              x_d[(tt + 2) * 128:(tt + 3) * 128, :],
                                              [m_resid[tt]])
                    defer_te(th_epi)
                barrier()
                flush()

        with ExitStack() as pc:
            atile = [SB(pc, f"atile{i}", [128, 8 * 128], BF16) for i in range(8)]
            m_at = {}
            with ExitStack() as pc1:
                x1T = SB(pc1, "x1T", [128, NCH * S], BF16)
                sg_sb = SB(pc1, "sg_sb", [128, 8 * S], BF16)
                wsl = [SB(pc1, f"wsl{i}", [128, NCH * NI], BF16) for i in range(2)]
                ringb = SB(pc1, "ringb", [128, 8 * 512], BF16)
                ring = [ringb[:, r * 512:(r + 1) * 512] for r in range(8)]
                identb = SB(pc1, "identb_sb", [128, 128], BF16)
                xrow = [SB(pc1, f"xrow{i}", [128, D], BF16) for i in range(2)]
                lg_pp = [PS(pc1, f"lg_pp{i}", [128, 512], F32) for i in range(2)]
                gu_pp = [PS(pc1, f"gu_pp{i}", [128, 512], F32) for i in range(4)]
                tpool = [PS(pc1, f"tpool{i}", [128, 512], F32) for i in range(2)]

                if debug:
                    dma("gpsimd", dbg["d_x1f"], x1f_scr[:, :], [m_x1f_w[NT - 1]])
                    dma("gpsimd", dbg["d_x1bf"], x1bf_scr[:, :], [m_x1bf_w[NT - 1]])
                m_wg = [None] * NE
                m_wu = [None] * NE
                m_wg0_ch = [None] * NCH
                m_wu0_ch = [None] * NCH
                m_idb = dma("sync", identb[:], identb_d)
                m_xrow = {}
                m_x1T_ev = {}
                tp_war = [None, None]
                tp_te_last = {}
                x1Tv = x1T.rearrange("p (c s) -> p c s", s=S)

                def emit_xrow(tt):
                    w = []
                    if tt >= 2:
                        w.append(tp_te_last[tt - 2])
                    m_xrow[tt] = dma("sync" if tt % 2 == 0 else "scalar",
                                     xrow[tt % 2][:],
                                     x1bf_scr[tt * 128:(tt + 1) * 128, :], w)

                emit_xrow(0)
                emit_xrow(1)
                for tt in range(NT):
                    for grp in range(4):
                        slot = (tt * 4 + grp) % 2
                        mm = None
                        for kk in range(4):
                            ch = grp * 4 + kk
                            w = []
                            if kk == 0:
                                w.append(tp_war[slot])
                                if grp == 0:
                                    w.append(m_xrow[tt])
                                    if tt == 0:
                                        w.append(m_idb)
                            def ftr(e, tt=tt, ch=ch, kk=kk, slot=slot):
                                return e.matmul(tpool[slot][:, kk * 128:(kk + 1) * 128],
                                                xrow[tt % 2][:, ch * 128:(ch + 1) * 128],
                                                identb[:], start=True, stop=True)
                            mm = em("tensor", ftr, w, "t", 1)
                        tp_te_last[tt] = mm
                        lt = ("t", mm[1] + 1)
                        dst = x1Tv[:, grp * 4:(grp + 1) * 4, tt * 128:(tt + 1) * 128]
                        if grp % 2 == 0:
                            def fe(e, dst=dst, slot=slot):
                                return e.activation(
                                    dst, tpool[slot][:].rearrange("p (c k) -> p c k", k=128),
                                    AF.Identity)
                            mev = em("scalar", fe, [lt], "a", 1)
                        else:
                            def fe(e, dst=dst, slot=slot):
                                return e.tensor_copy(
                                    dst, tpool[slot][:].rearrange("p (c k) -> p c k", k=128))
                            mev = em("vector", fe, [lt], "v", 1)
                        m_x1T_ev[(tt, grp)] = mev
                        tp_war[slot] = mev
                    if tt + 2 < NT:
                        emit_xrow(tt + 2)
                    m_wg0_ch[tt] = dma("sync" if tt % 2 == 0 else "scalar",
                                       wsl[0][:, tt * NI:(tt + 1) * NI],
                                       wg0_scr[:, tt * NI:(tt + 1) * NI])
                em("tensor", lambda e: e.drain(), [], "t", 1)
                for c in range(NCH):
                    m_wu0_ch[c] = dma("sync" if c % 2 == 0 else "scalar",
                                      wsl[1][:, c * NI:(c + 1) * NI],
                                      wu0_scr[:, c * NI:(c + 1) * NI])
                m_wg[0] = m_wg0_ch[-1]
                m_wu[0] = m_wu0_ch[-1]

                m_lg_ev = [None] * NT

                def emit_logits(tt):
                    mm = None
                    w = []
                    if tt >= 2:
                        w.append(m_lg_ev[tt - 2])
                    for ch in range(NCH):
                        wc = list(w) if ch == 0 else []
                        if tt == 0:
                            wc.append(m_gwt)
                        def f(e, tt=tt, ch=ch):
                            return e.matmul(lg_pp[tt % 2][:, 0:8],
                                            x1T[:, ch * S + tt * 128: ch * S + tt * 128 + 128],
                                            gwt[:, ch * 8:(ch + 1) * 8],
                                            start=(ch == 0), stop=(ch == NCH - 1))
                        mm = em("tensor", f, wc if wc or ch == 0 else [],
                                "t" if ch in (0, NCH - 1) else None, 1)
                    def th_cmb(mil, tt=tt, mm=mm):
                        def fx(e, tt=tt):
                            return e.activation(el_all[:, tt * 8:(tt + 1) * 8],
                                                lg_pp[tt % 2][:, 0:8], AF.Exp)
                        m_lg_ev[tt] = em("scalar", fx, [mm, mil], "a", 1)
                        el = el_all[:, tt * 8:(tt + 1) * 8]
                        em("vector", lambda e, el=el: e.reduce_max(m1t[:], el, axis=AX.X),
                           [m_lg_ev[tt]], "v", 1)
                        em("vector", lambda e, el=el: e.tensor_scalar(
                            t8a[:], el, m1t[:, 0:1], None, ALU.is_equal), [], "v", 1)
                        em("vector", lambda e, el=el: e.scalar_tensor_tensor(
                            t8b[:], t8a[:], -1e30, el, ALU.mult, ALU.add), [], "v", 1)
                        em("vector", lambda e: e.reduce_max(m2t[:], t8b[:], axis=AX.X),
                           [], "v", 1)
                        em("vector", lambda e, el=el: e.tensor_scalar(
                            t8a[:], el, m2t[:, 0:1], None, ALU.is_ge), [], "v", 1)
                        em("vector", lambda e, el=el: e.tensor_mul(t8b[:], el, t8a[:]),
                           [], "v", 1)
                        em("vector", lambda e: e.tensor_add(s12t[:], m1t[:], m2t[:]),
                           [], "v", 1)
                        em("vector", lambda e: e.reciprocal(rsv[:], s12t[:]), [], "v", 1)
                        em("vector", lambda e, tt=tt: e.tensor_scalar(
                            c_all[:, tt * 8:(tt + 1) * 8], t8b[:], rsv[:, 0:1],
                            None, ALU.mult), [], "v", 1)
                    defer_te(th_cmb)

                m_gu_ev = {}
                m_am = {}
                m_aw = {}
                m_gpass_done = [None] * NE
                m_upass_done = [None] * NE
                gu_jc = [0]

                def emit_gu_pass(e_i, is_u):
                    for ic in range(8):
                        for tc in range(4):
                            j = gu_jc[0]
                            gu_jc[0] += 1
                            slot = wsl[is_u]
                            m_wready = (m_wu if is_u else m_wg)[e_i]
                            w = [m_wready] if (ic == 0 and tc == 0 and e_i > 0) else []
                            if j >= 4:
                                w.append(m_gu_ev[j - 4])
                            mm = None
                            for ch in range(NCH):
                                wch = list(w) if ch == 0 else []
                                if e_i == 0 and ic == 0:
                                    wch.append((m_wu0_ch if is_u else m_wg0_ch)[ch])
                                    if not is_u:
                                        for i4 in range(4):
                                            wch.append(m_x1T_ev[(4 * tc + i4, ch // 4)])
                                def f(e, j=j, ic=ic, tc=tc, ch=ch, slot=slot):
                                    return e.matmul(
                                        gu_pp[j % 4][:],
                                        slot[:, ch * NI + ic * 128: ch * NI + (ic + 1) * 128],
                                        x1T[:, ch * S + tc * 512: ch * S + (tc + 1) * 512],
                                        start=(ch == 0), stop=(ch == NCH - 1))
                                mm = em("tensor", f, wch,
                                        "t" if ch in (0, NCH - 1) else None, 1)
                            if not is_u:
                                wv = [mm]
                                if e_i > 0:
                                    wv.append(m_am.get((e_i - 1, ic, tc)))
                                def fs(e, j=j, ic=ic, tc=tc):
                                    return e.activation(
                                        sg_sb[:, ic * S + tc * 512: ic * S + (tc + 1) * 512],
                                        gu_pp[j % 4][:], AF.Silu)
                                def th_sg(mil, fs=fs, wv=wv, j=j):
                                    m_gu_ev[j] = em("scalar", fs, wv + [mil], "a", 1)
                                defer_te(th_sg)
                            else:
                                r = (ic * 4 + tc) % 8
                                wv = [mm]
                                prev = m_aw.get((e_i, ic * 4 + tc - 8)) if ic * 4 + tc >= 8 \
                                    else (m_aw.get((e_i - 1, 24 + ic * 4 + tc)) if e_i > 0 else None)
                                if prev is not None:
                                    wv.append(prev)
                                def fa(e, j=j, ic=ic, tc=tc, r=r):
                                    return e.tensor_mul(
                                        ring[r][:],
                                        sg_sb[:, ic * S + tc * 512: ic * S + (tc + 1) * 512],
                                        gu_pp[j % 4][:])
                                def th_am(mil, fa=fa, wv=wv, j=j, e_i=e_i,
                                          ic=ic, tc=tc, r=r):
                                    m_am[(e_i, ic, tc)] = em("vector", fa,
                                                             wv + [mil], "v", 1)
                                    m_gu_ev[j] = m_am[(e_i, ic, tc)]
                                    m_aw[(e_i, ic * 4 + tc)] = dma(
                                        "sync",
                                        ats_scr[e_i, ic, :, tc * 512:(tc + 1) * 512],
                                        ring[r][:], [m_am[(e_i, ic, tc)]])
                                defer_te(th_am)
                    if not is_u:
                        m_gpass_done[e_i] = mm
                    else:
                        m_upass_done[e_i] = mm

                emit_gu_pass(0, 0)
                if debug:
                    dma("gpsimd", dbg["d_x1T"], x1T[:], [m_gpass_done[0]])
                for tt in range(NT):
                    emit_logits(tt)
                emit_gu_pass(0, 1)
                m_wg[1] = dma("gpsimd", wsl[0][:], wg_d[1], [m_gpass_done[0]])
                m_wu[1] = dma("gpsimd", wsl[1][:], wu_d[1], [m_upass_done[0]])
                for e_i in range(1, NE):
                    emit_gu_pass(e_i, 0)
                    emit_gu_pass(e_i, 1)
                    if e_i + 1 < NE:
                        m_wg[e_i + 1] = dma("gpsimd", wsl[0][:], wg_d[e_i + 1],
                                            [m_gpass_done[e_i]])
                        m_wu[e_i + 1] = dma("gpsimd", wsl[1][:], wu_d[e_i + 1],
                                            [m_upass_done[e_i]])
                aw_last = m_aw[(0, 31)]
                for k in range(8):
                    m_at[(0, k)] = dma(
                        "sync" if k % 2 == 0 else "scalar",
                        atile[k][:].rearrange("p (ic t) -> p ic t", t=128),
                        ats_scr[0, :, :, k * 128:(k + 1) * 128].rearrange(
                            "ic p t -> p ic t"), [aw_last])
                barrier()
                flush()

            with ExitStack() as pc2:
                acc = SB(pc2, "acc", [128, NT * D], BF16)
                wdsl = [SB(pc2, f"wdsl{i}", [128, 8 * D], BF16) for i in range(2)]
                x1fr = [SB(pc2, f"x1fr{i}", [128, D], F32) for i in range(2)]
                resid2 = SB(pc2, "resid2", [128, D], F32)
                out_sb = [SB(pc2, f"out_sb{i}", [128, D], F32) for i in range(2)]
                y_pp = [PS(pc2, f"y_pp{i}", [128, D], F32) for i in range(2)]

                if debug:
                    dma("gpsimd", dbg["d_el"], el_all[:])
                    dma("gpsimd", dbg["d_call"], c_all[:])
                    dma("gpsimd", dbg["d_at0"], ats_scr[0])
                m_wd = [None] * NE
                m_wd0_ic = [dma("sync" if ic % 2 == 0 else "scalar",
                                wdsl[0][:, ic * D:(ic + 1) * D],
                                wd0_scr[:, ic * D:(ic + 1) * D]) for ic in range(8)]
                m_wd[0] = m_wd0_ic[-1]
                m_wd[1] = dma("gpsimd", wdsl[1][:], wd_d[1])
                m_ymm = {}
                m_stt = {}
                m_x1r = [None] * NT
                m_outw = [None] * NT
                m_outw2 = [None] * NT
                m_ln2 = [None] * NT
                for k in range(2):
                    m_x1r[k] = dma("sync", x1fr[k][:], x1f_scr[k * 128:(k + 1) * 128, :])
                for e_i in range(NE):
                    for tt in range(NT):
                        w = [m_at[(e_i, tt)], m_wd[e_i]]
                        mm = None
                        for dc in range(4):
                            wdc = list(w) if dc == 0 else []
                            if dc == 0:
                                prev = m_stt.get((e_i - 1, tt)) if e_i > 0 else None
                                pv = m_stt.get((e_i, tt - 2)) or \
                                    (m_stt.get((e_i - 1, NT - 2 + tt)) if e_i > 0 else None)
                                if pv is not None:
                                    wdc.append(pv)
                                if prev is not None:
                                    wdc.append(prev)
                            for ic in range(8):
                                wic = wdc if ic == 0 else []
                                if e_i == 0 and tt == 0 and dc == 0:
                                    wic = list(wic) + [m_wd0_ic[ic]]
                                    if ic == 0:
                                        wic.remove(m_wd[0])
                                def f(e, e_i=e_i, tt=tt, dc=dc, ic=ic):
                                    return e.matmul(
                                        y_pp[tt % 2][:, dc * 512:(dc + 1) * 512],
                                        atile[tt % 8][:, ic * 128:(ic + 1) * 128],
                                        wdsl[e_i % 2][:, ic * D + dc * 512: ic * D + (dc + 1) * 512],
                                        start=(ic == 0), stop=(ic == 7))
                                mm = em("tensor", f, wic,
                                        "t" if ic in (0, 7) else None, 1)
                            m_ymm[(e_i, tt, dc)] = mm
                            cs = c_all[:, tt * 8 + e_i: tt * 8 + e_i + 1]
                            dst = acc[:, tt * D + dc * 512: tt * D + (dc + 1) * 512]
                            if e_i == 0:
                                def facc(e, dst=dst, cs=cs, tt=tt, dc=dc):
                                    return e.tensor_scalar(
                                        dst, y_pp[tt % 2][:, dc * 512:(dc + 1) * 512],
                                        cs, None, ALU.mult)
                            else:
                                def facc(e, dst=dst, cs=cs, tt=tt, dc=dc):
                                    return e.scalar_tensor_tensor(
                                        dst, y_pp[tt % 2][:, dc * 512:(dc + 1) * 512],
                                        cs, dst, ALU.mult, ALU.add)

                            def th_stt(mil, facc=facc, mm=mm, e_i=e_i, tt=tt, dc=dc):
                                m_stt[(e_i, tt)] = em("vector", facc, [mm, mil], "v", 1)
                                if not (dc == 3 and e_i == NE - 1):
                                    return
                                c = tt % 2
                                w3 = [m_stt[(e_i, tt)], m_x1r[tt]]
                                def g1(e, tt=tt, c=c):
                                    return e.tensor_add(resid2[:],
                                                        acc[:, tt * D:(tt + 1) * D],
                                                        x1fr[c][:])
                                mr2 = em("vector", g1, w3, "v", 1)
                                wsq = [mr2]
                                if tt >= 2 and m_outw[tt - 2] is not None:
                                    wsq.append(m_outw[tt - 2])
                                    wsq.append(m_outw2[tt - 2])
                                def g2(e, c=c):
                                    return e.activation(out_sb[c][:], resid2[:],
                                                        AF.Identity,
                                                        accum_out=sm["sum_x"][:, c:c + 1])
                                em("scalar", g2, wsq, "ar", 1)
                                def g3(e, c=c):
                                    return e.activation(out_sb[c][:], resid2[:],
                                                        AF.Square,
                                                        accum_out=sm["ssq"][:, c:c + 1])
                                em("scalar", g3, [], "ar", 1)
                                def g4(e, c=c):
                                    return e.activation(sm["mvec"][:, c:c + 1],
                                                        sm["sum_x"][:, c:c + 1],
                                                        AF.Identity, scale=1.0 / D)
                                em("scalar", g4, [], "ar", 1)
                                def g5(e, c=c):
                                    return e.activation(sm["msq"][:, c:c + 1],
                                                        sm["mvec"][:, c:c + 1],
                                                        AF.Square)
                                em("scalar", g5, [], "ar", 1)
                                def g6(e, c=c):
                                    return e.activation(sm["t1v"][:, c:c + 1],
                                                        sm["ssq"][:, c:c + 1],
                                                        AF.Identity, scale=1.0 / D)
                                m_pre = em("scalar", g6, [], "a", 1)
                                def g7(e, c=c):
                                    return e.scalar_tensor_tensor(
                                        sm["varv"][:, c:c + 1], sm["t1v"][:, c:c + 1],
                                        EPS, sm["msq"][:, c:c + 1],
                                        ALU.add, ALU.subtract)
                                m_var = em("vector", g7, [m_pre], "v", 1)
                                def g8(e, c=c):
                                    return e.activation(sm["lnv"][:, c:c + 1],
                                                        sm["varv"][:, c:c + 1], AF.Ln)
                                em("scalar", g8, [m_var], "a", 1)
                                def g9(e, c=c):
                                    return e.activation(sm["rstd"][:, c:c + 1],
                                                        sm["lnv"][:, c:c + 1],
                                                        AF.Exp, scale=-0.5)
                                m_rstd = em("scalar", g9, [], "a", 1)
                                def g10(e, c=c):
                                    return e.tensor_scalar(out_sb[c][:], resid2[:],
                                                           sm["mvec"][:, c:c + 1],
                                                           sm["rstd"][:, c:c + 1],
                                                           ALU.subtract, ALU.mult)
                                m_ln2[tt] = em("vector", g10, [m_rstd], "v", 1)
                                def gout(e, tt=tt, c=c):
                                    return e.dma_start(out_d[tt * 128:(tt + 1) * 128,
                                                             0:D // 2],
                                                       out_sb[c][:, 0:D // 2])
                                mo1 = em("gpsimd", gout, [m_ln2[tt]], "do", 16)
                                mo2 = dma("scalar",
                                          out_d[tt * 128:(tt + 1) * 128, D // 2:D],
                                          out_sb[c][:, D // 2:D], [m_ln2[tt]])
                                m_outw[tt] = mo1
                                m_outw2[tt] = mo2
                                if tt + 2 < NT:
                                    m_x1r[tt + 2] = dma(
                                        "sync", x1fr[c][:],
                                        x1f_scr[(tt + 2) * 128:(tt + 3) * 128, :],
                                        [mr2])
                            defer_te(th_stt)
                        ne_, nt_ = (e_i, tt + 8) if tt + 8 < NT else \
                            ((e_i + 1, tt + 8 - NT) if e_i + 1 < NE else (None, None))
                        if ne_ is not None:
                            m_at[(ne_, nt_)] = dma(
                                "sync" if nt_ % 2 == 0 else "scalar",
                                atile[tt % 8][:].rearrange("p (ic t) -> p ic t", t=128),
                                ats_scr[ne_, :, :, nt_ * 128:(nt_ + 1) * 128].rearrange(
                                    "ic p t -> p ic t"),
                                [m_ymm[(e_i, tt, 3)]])
                    if e_i + 2 < NE:
                        m_wd[e_i + 2] = dma("gpsimd", wdsl[e_i % 2][:],
                                            wd_d[e_i + 2],
                                            [m_ymm[(e_i, NT - 1, 3)]])
                    if e_i == 0:
                        pass
                te_tail()
                if debug:
                    dma("gpsimd", dbg["d_acc"], acc[:], [m_stt[(NE - 1, NT - 1)]])
                em("gpsimd", None, [("do", cnt["do"]), ("dg", cnt["dg"]),
                                    ("ds", cnt["ds"]), ("dc", cnt.get("dc", 0))])
                flush()
    return nc


_NC_CACHE = None


def kernel(L0_attn_x, L0_attn_W_qkv, L0_attn_b_qkv, L0_attn_W_out, L0_attn_b_out,
           L0_ffn_gate_w, L0_ffn_w_gate, L0_ffn_w_up, L0_ffn_w_down):
    global _NC_CACHE
    f32 = np.float32
    x = np.asarray(L0_attn_x, f32)
    wqkv = np.asarray(L0_attn_W_qkv, f32)
    bqkv = np.asarray(L0_attn_b_qkv, f32)
    wout = np.asarray(L0_attn_W_out, f32)
    bout = np.asarray(L0_attn_b_out, f32)
    gw = np.asarray(L0_ffn_gate_w, f32)
    wg = np.asarray(L0_ffn_w_gate, f32)
    wu = np.asarray(L0_ffn_w_up, f32)
    wd = np.asarray(L0_ffn_w_down, f32)

    wqkvt = np.ascontiguousarray(
        wqkv.reshape(16, 128, 16, 384).transpose(2, 1, 0, 3)).reshape(16, 128, 16 * 384)
    bqkvt = np.ascontiguousarray(bqkv.reshape(48, 128).T)
    woutt = np.ascontiguousarray(
        wout.reshape(16, 128, 2048).transpose(1, 0, 2)).reshape(128, 16 * 2048)
    boutr = bout.reshape(1, 2048)
    gwt = np.ascontiguousarray(gw.T.reshape(16, 128, 8).transpose(1, 0, 2)).reshape(128, 128)
    wgt = np.ascontiguousarray(
        wg.reshape(8, 16, 128, 1024).transpose(0, 2, 1, 3)).reshape(8, 128, 16 * 1024)
    wut = np.ascontiguousarray(
        wu.reshape(8, 16, 128, 1024).transpose(0, 2, 1, 3)).reshape(8, 128, 16 * 1024)
    wdt = np.ascontiguousarray(
        wd.reshape(8, 8, 128, 2048).transpose(0, 2, 1, 3)).reshape(8, 128, 8 * 2048)

    kp = np.arange(128)[:, None]
    qf = np.arange(512)[None, :]
    masks = np.concatenate([(qf >= kp + 128 * r) for r in range(4)], axis=1)
    masks = masks.astype(np.float32).astype(BF)
    ones_col = np.ones((128, 1), np.float32).astype(BF)
    ones_row = np.ones((1, 512), np.float32).astype(BF)

    if _NC_CACHE is None:
        _NC_CACHE = build_nc()
    nc = _NC_CACHE

    common = {
        "wqkv": wqkvt, "bqkvt": bqkvt, "woutt": woutt, "bout": boutr,
        "gwt": gwt, "wgt": wgt, "wut": wut, "wdt": wdt,
        "masks": masks, "ones_col": ones_col, "ones_row": ones_row,
        "ones_sq": np.ones((128, 128), np.float32).astype(BF),
        "ident": np.eye(128, dtype=np.float32),
        "identb": np.eye(128, dtype=np.float32).astype(BF),
    }
    in_maps = [dict(common, x=np.ascontiguousarray(x[b])) for b in range(8)]
    res = run_bass_kernel_spmd(nc, in_maps, core_ids=list(range(8)))
    return np.stack([res.results[b]["out"] for b in range(8)], axis=0)
